# version 02
# baseline (speedup 1.0000x reference)
import contextlib
import numpy as np

import concourse.bass as bass
import concourse.mybir as mybir
from concourse.bass_utils import run_bass_kernel_spmd

F32 = mybir.dt.float32
BF16 = mybir.dt.bfloat16
I32 = mybir.dt.int32
ALU = mybir.AluOpType
AX = mybir.AxisListType
AF = mybir.ActivationFunctionType
IOA = bass.IndirectOffsetOnAxis

NCORES = 8
T = 2048
D = 1024
DI = 2048
NCH = D // 128
NCHI = DI // 128
TT = 512
NTT = T // TT
CAPE = 512
NEXP = 8
EPS = 1e-5
DEBUG_MOE = False


ENGS = ("pe", "dve", "act", "pool", "sync")


class _Op:
    __slots__ = ("eng", "emit", "deps", "inc", "milestone", "val_after",
                 "waits", "tag", "fsync", "eseq", "small")

    def __init__(self, eng, emit, deps, inc, tag=""):
        self.eng, self.emit, self.deps, self.inc = eng, emit, deps, inc
        self.milestone = False
        self.val_after = 0
        self.waits = []
        self.tag = tag
        self.fsync = False
        self.small = False


class Ref:
    __slots__ = ("tb", "p0", "p1", "f0", "f1", "apfn")

    def __init__(self, tb, p0, p1, f0, f1, apfn=None):
        self.tb, self.p0, self.p1, self.f0, self.f1 = tb, p0, p1, f0, f1
        self.apfn = apfn

    @property
    def ap(self):
        if self.apfn is not None:
            return self.apfn()
        return self.tb.h.ap()[self.p0:self.p1, self.f0:self.f1]


class TB:

    def __init__(self, prog, handle, shape, name):
        self.prog, self.h, self.shape, self.name = prog, handle, shape, name
        self.accs = []

    def __getitem__(self, key):
        if not isinstance(key, tuple):
            key = (key, slice(None))
        ps, fs = key
        p0, p1, _ = ps.indices(self.shape[0]) if isinstance(ps, slice) else (ps, ps + 1, 1)
        f0, f1, _ = fs.indices(self.shape[1]) if isinstance(fs, slice) else (fs, fs + 1, 1)
        return Ref(self, p0, p1, f0, f1)

    @property
    def full(self):
        return self[:, :]

    def rv(self, apfn):
        return Ref(self, 0, self.shape[0], 0, self.shape[1], apfn)


class Prog:
    def __init__(self, nc):
        self.nc = nc
        self.ops = []
        self.by_eng = {e: [] for e in ENGS}
        self.ctx = contextlib.ExitStack()
        self.sems = {}
        self.const_tb = None
        self.const_map = {}
        self.const_n = 0
        self._bregs = {}

    def sbuf(self, name, shape, dtype):
        h = self.ctx.enter_context(self.nc.sbuf_tensor(name, list(shape), dtype))
        return TB(self, h, tuple(shape), name)

    def psum(self, name, shape, dtype=F32):
        h = self.ctx.enter_context(self.nc.psum_tensor(name, list(shape), dtype))
        return TB(self, h, tuple(shape), name)

    def dram(self, name, shape, dtype, kind="Internal"):
        h = self.nc.dram_tensor(name, list(shape), dtype, kind=kind)
        return TB(self, h, tuple(shape), name)

    def _touch(self, ref, is_write, op):
        tb = ref.tb
        deps = []
        if is_write:
            keep = []
            for rec in tb.accs:
                o, w, p0, p1, f0, f1 = rec
                if p0 < ref.p1 and ref.p0 < p1 and f0 < ref.f1 and ref.f0 < f1:
                    deps.append(o)
                    if ref.p0 <= p0 and p1 <= ref.p1 and ref.f0 <= f0 and f1 <= ref.f1:
                        continue
                keep.append(rec)
            tb.accs = keep
        else:
            for rec in tb.accs:
                o, w, p0, p1, f0, f1 = rec
                if w and p0 < ref.p1 and ref.p0 < p1 and f0 < ref.f1 and ref.f0 < f1:
                    deps.append(o)
        tb.accs.append((op, is_write, ref.p0, ref.p1, ref.f0, ref.f1))
        return deps

    def add(self, eng, emit, reads=(), writes=(), deps=(), dma=False, tag="",
            fsync=False):
        op = _Op(eng, emit, list(deps), 16 if dma else 1, tag)
        op.fsync = fsync
        reads = [r.full if isinstance(r, TB) else r for r in reads]
        writes = [w.full if isinstance(w, TB) else w for w in writes]
        if writes and not dma:
            op.small = max(w.f1 - w.f0 for w in writes) < 256
        for r in reads:
            op.deps.extend(self._touch(r, False, op))
        for w in writes:
            op.deps.extend(self._touch(w, True, op))
        op.deps = [d for d in op.deps if d is not op]
        if dma:
            op.milestone = True
        self.ops.append(op)
        self.by_eng[eng].append(op)
        return op

    def dma(self, eng, dst, src, deps=(), tag="dma"):
        return self.add(eng, lambda e, d=dst, s=src: e.dma_start(out=d.ap, in_=s.ap),
                        reads=[src], writes=[dst], deps=deps, dma=True, tag=tag)

    def tt(self, out, a, b, op, eng="dve", deps=(), tag="tt", fs=False):
        return self.add(eng, lambda e, o=out, x=a, y=b, al=op:
                        e.tensor_tensor(out=o.ap, in0=x.ap, in1=y.ap, op=al),
                        reads=[a, b], writes=[out], deps=deps, tag=tag, fsync=fs)

    def ts(self, out, a, s1, s2=None, op0=ALU.mult, op1=ALU.bypass, eng="dve",
           deps=(), tag="ts", fs=False):
        if isinstance(s1, Ref) and (out.f1 - out.f0) > 128:
            raise AssertionError(f"AP-scalar tensor_scalar too wide: {tag}")
        rd = [a] + [s for s in (s1, s2) if isinstance(s, Ref)]
        if s2 is None:
            return self.add(eng, lambda e, o=out, x=a,
                            sv=(s1.ap if isinstance(s1, Ref) else s1):
                            e.tensor_single_scalar(out=o.ap, in_=x.ap, scalar=sv,
                                                   op=op0),
                            reads=rd, writes=[out], deps=deps, tag=tag, fsync=fs)
        return self.add(eng, lambda e, o=out, x=a,
                        sv=(s1.ap if isinstance(s1, Ref) else s1),
                        sw=(s2.ap if isinstance(s2, Ref) else s2):
                        e.tensor_scalar(out=o.ap, in0=x.ap, scalar1=sv, scalar2=sw,
                                        op0=op0, op1=op1),
                        reads=rd, writes=[out], deps=deps, tag=tag, fsync=fs)

    def stt(self, out, a, scalar, b, op0, op1, eng="dve", deps=(), tag="stt", fs=False):
        deps = list(deps)
        if isinstance(scalar, float) and scalar not in (0.0, 1.0):
            deps.append(self.constref(scalar)[1])
        rd = [a, b] + ([scalar] if isinstance(scalar, Ref) else [])
        return self.add(eng, lambda e, o=out, x=a, y=b,
                        sv=(scalar.ap if isinstance(scalar, Ref) else scalar):
                        e.scalar_tensor_tensor(out=o.ap, in0=x.ap, scalar=sv,
                                               in1=y.ap, op0=op0, op1=op1),
                        reads=rd, writes=[out], deps=deps, tag=tag, fsync=fs)

    def constref(self, v):
        v = float(v)
        if v not in self.const_map:
            if self.const_tb is None:
                self.const_tb = self.sbuf("constp", (1, 16), F32)
            j = self.const_n
            self.const_n += 1
            r = self.const_tb[0:1, j:j + 1]
            op = self.memset(r, v, tag="const")
            self.nc.const_aps.register(F32, v, r.ap)
            self.const_map[v] = (r, op)
        return self.const_map[v]

    def act(self, out, a, func, bias=0.0, scale=1.0, deps=(), tag="act",
            fs=False):
        deps = list(deps)
        for s in (bias, scale):
            if isinstance(s, float) and s not in (0.0, 1.0):
                deps.append(self.constref(s)[1])
        rd = [a] + [s for s in (bias, scale) if isinstance(s, Ref)]
        return self.add("act", lambda e, o=out, x=a,
                        bv=(bias.ap if isinstance(bias, Ref) else bias),
                        sv=(scale.ap if isinstance(scale, Ref) else scale):
                        e.activation(out=o.ap, in_=x.ap, func=func, bias=bv,
                                     scale=sv),
                        reads=rd, writes=[out], deps=deps, tag=tag, fsync=fs)

    def relu(self, out, a, deps=(), tag="relu", fs=False):
        return self.add("dve", lambda e, o=out, x=a:
                        e.tensor_relu(out=o.ap, in_=x.ap),
                        reads=[a], writes=[out], deps=deps, tag=tag, fsync=fs)

    def copy(self, out, a, eng="dve", deps=(), tag="copy", fs=False):
        return self.add(eng, lambda e, o=out, x=a: e.tensor_copy(out=o.ap, in_=x.ap),
                        reads=[a], writes=[out], deps=deps, tag=tag, fsync=fs)

    def memset(self, out, val, eng="dve", deps=(), tag="memset", fs=False):
        return self.add(eng, lambda e, o=out, v=val: e.memset(o.ap, v),
                        writes=[out], deps=deps, tag=tag, fsync=fs)

    def red(self, out, a, op=ALU.max, deps=(), tag="red", fs=False):
        return self.add("dve", lambda e, o=out, x=a:
                        e.tensor_reduce(out=o.ap, in_=x.ap, axis=AX.X, op=op),
                        reads=[a], writes=[out], deps=deps, tag=tag, fsync=fs)

    def scan(self, out, d0, d1, initial, op0, op1, deps=(), tag="scan"):
        rd = [d0, d1] + ([initial] if isinstance(initial, Ref) else [])
        return self.add("dve", lambda e, o=out, a=d0, b=d1,
                        iv=(initial.ap if isinstance(initial, Ref) else initial):
                        e.tensor_tensor_scan(out=o.ap, data0=a.ap, data1=b.ap,
                                             initial=iv, op0=op0, op1=op1),
                        reads=rd, writes=[out], deps=deps, tag=tag, fsync=True)

    def mm(self, out, lhsT, rhs, start, stop, deps=(), tag="mm"):
        return self.add("pe", lambda e, o=out, l=lhsT, r=rhs, s0=start, s1=stop:
                        e.matmul(o.ap, l.ap, r.ap, start=s0, stop=s1),
                        reads=[lhsT, rhs], writes=[out], deps=deps, tag=tag)

    def transpose(self, out, a, ident, deps=(), tag="tr"):
        return self.add("pe", lambda e, o=out, x=a, i=ident:
                        e.transpose(o.ap, x.ap, i.ap),
                        reads=[a, ident], writes=[out], deps=deps, tag=tag)

    def recip(self, out, a, deps=(), tag="recip", fs=False):
        return self.add("dve", lambda e, o=out, x=a: e.reciprocal(out=o.ap, in_=x.ap),
                        reads=[a], writes=[out], deps=deps, tag=tag, fsync=fs)

    def igather(self, out, table, idx, bound=None, deps=(), tag="ig"):
        def emit(e, o=out, t=table, i=idx, b=bound):
            tap = t.h.ap() if isinstance(t, TB) else t.ap
            if b is None:
                return e.indirect_dma_start(out=o.ap, out_offset=None, in_=tap,
                                            in_offset=IOA(i.ap, 0))
            breg = self.breg(e, b)
            return e.indirect_dma_start(out=o.ap, out_offset=None, in_=tap,
                                        in_offset=IOA(i.ap, 0), bounds_check=breg,
                                        oob_is_err=False)
        return self.add("pool", emit, reads=[table, idx], writes=[out], deps=deps,
                        dma=True, tag=tag)

    def breg(self, e, val):
        key = (id(e), val)
        if key not in self._bregs:
            self._bregs[key] = e.to_reg(val)
        return self._bregs[key]

    def iscatter(self, table, src, idx, bound, deps=(), tag="is"):
        def emit(e, t=table, s=src, i=idx, b=bound):
            breg = self.breg(e, b)
            tap = t.h.ap() if isinstance(t, TB) else t.ap
            return e.indirect_dma_start(out=tap, out_offset=IOA(i.ap, 0), in_=s.ap,
                                        in_offset=None, bounds_check=breg,
                                        oob_is_err=False)
        return self.add("pool", emit, reads=[src, idx], writes=[table], deps=deps,
                        dma=True, tag=tag)

    GAP = 0

    def _needs_edge(self, op, d):
        if d.eng != op.eng or d.inc == 16 or d.fsync or op.fsync:
            return True
        return d.small and (op.eseq - d.eseq) < self.GAP

    def lower_and_emit(self):
        nc = self.nc
        for e in ENGS:
            self.sems[e] = self.ctx.enter_context(nc.semaphore("sem_" + e))
        for e in ENGS:
            for i, op in enumerate(self.by_eng[e]):
                op.eseq = i
        for op in self.ops:
            latest = {}
            for d in op.deps:
                if d is op or not self._needs_edge(op, d):
                    continue
                cur = latest.get(d.eng)
                if cur is None or d.eseq > cur.eseq:
                    latest[d.eng] = d
            op.deps = list(latest.values())
            for d in op.deps:
                d.milestone = True
        vals = {e: 0 for e in ENGS}
        for e in ENGS:
            for op in self.by_eng[e]:
                if op.milestone:
                    vals[e] += op.inc
                    op.val_after = vals[e]
        for e in ENGS:
            seen = {x: 0 for x in ENGS}
            for op in self.by_eng[e]:
                need = {}
                for d in op.deps:
                    need[d.eng] = max(need.get(d.eng, 0), d.val_after)
                for se, v in sorted(need.items()):
                    if v > seen[se]:
                        op.waits.append((se, v))
                        seen[se] = v
        finals = dict(vals)
        block = self.ctx.enter_context(nc.Block())

        def runner(eng_name):
            def run(engine):
                for op in self.by_eng[eng_name]:
                    for se, v in op.waits:
                        engine.wait_ge(self.sems[se], v)
                    inst = op.emit(engine)
                    if op.milestone:
                        inst.then_inc(self.sems[op.eng], op.inc)
                if eng_name == "sync":
                    for de in ("sync", "pool"):
                        if finals[de]:
                            engine.wait_ge(self.sems[de], finals[de])
            return run

        block.tensor(runner("pe"))
        block.vector(runner("dve"))
        block.scalar(runner("act"))
        block.gpsimd(runner("pool"))
        block.sync(runner("sync"))
        self.ctx.close()



def _host_constants():
    j = np.arange(D, dtype=np.int64)
    th = 2.0 * np.pi * ((np.outer(j, j) % D).astype(np.float64)) / D
    tC = np.cos(th).astype(np.float32)
    tS = np.sin(th).astype(np.float32)
    tCn = (-tC).astype(np.float32)
    eye = np.eye(128, dtype=np.float32)
    pri16 = np.tile((8.0 - np.arange(8, dtype=np.float32)), 16)[None, :] \
        .repeat(128, 0).copy()
    tokid = (np.arange(16)[None, :] * 128 + np.arange(128)[:, None]) \
        .astype(np.float32)
    return {"tC": tC, "tS": tS, "tCn": tCn, "eyef": eye, "pri16": pri16,
            "tokid": tokid}


def _chunk_cols(v, nch):
    v = np.asarray(v, np.float32).reshape(-1)
    return np.ascontiguousarray(v.reshape(nch, 128).T)


def _prep_inputs(I):
    A = lambda x: np.ascontiguousarray(np.asarray(x, np.float32))
    base = dict(_host_constants())
    base["w_in"] = A(I["L0_attn_W_in"])
    base["w_dt"] = A(I["L0_attn_W_dt"])
    base["w_x"] = A(I["L0_attn_W_x"])
    base["w_out"] = A(I["L0_attn_W_out"])
    base["ln_g"] = _chunk_cols(I["L0_attn_ln_g"], NCH)
    base["ln_b"] = _chunk_cols(I["L0_attn_ln_b"], NCH)
    base["conv_b"] = _chunk_cols(I["L0_attn_conv_b"], NCHI)
    base["b_dt"] = _chunk_cols(I["L0_attn_b_dt"], NCHI)
    base["d_vec"] = _chunk_cols(I["L0_attn_D"], NCHI)
    cw = np.asarray(I["L0_attn_conv_w"], np.float32) \
        .reshape(NCHI, 128, 4).transpose(1, 0, 2).reshape(128, NCHI * 4)
    base["conv_w"] = np.ascontiguousarray(cw)
    base["a_log"] = A(I["L0_attn_A_log"]).reshape(1, 64)
    base["wfr"] = A(I["L0_ffn_w_freq_real"])
    base["wfi"] = A(I["L0_ffn_w_freq_imag"])
    base["fbias"] = _chunk_cols(I["L0_ffn_bias"], NCH)
    base["wk_t"] = A(np.asarray(I["L1_attn_Wk"]).T)
    base["wv_t"] = A(np.asarray(I["L1_attn_Wv"]).T)
    base["wr_t"] = A(np.asarray(I["L1_attn_Wr"]).T)
    base["wo_t"] = A(np.asarray(I["L1_attn_Wo"]).T)
    base["tdec"] = _chunk_cols(I["L1_attn_time_decay"], NCH)
    base["tfirst"] = _chunk_cols(I["L1_attn_time_first"], NCH)
    base["tmk"] = _chunk_cols(np.asarray(I["L1_attn_time_mix_k"]).reshape(-1), NCH)
    base["tmv"] = _chunk_cols(np.asarray(I["L1_attn_time_mix_v"]).reshape(-1), NCH)
    base["tmr"] = _chunk_cols(np.asarray(I["L1_attn_time_mix_r"]).reshape(-1), NCH)
    base["rw"] = A(I["L1_ffn_router_w"])
    base["rb16"] = np.ascontiguousarray(
        np.tile(np.asarray(I["L1_ffn_router_b"], np.float32), 16)[None, :])
    base["moe_w1"] = A(I["L1_ffn_w1"]).reshape(NEXP * D, 4096)
    base["moe_w2"] = A(I["L1_ffn_w2"]).reshape(NEXP * 4096, D)
    b1 = np.asarray(I["L1_ffn_b1"], np.float32).reshape(NEXP, 32, 128) \
        .transpose(2, 0, 1).reshape(128, NEXP * 32)
    base["b1c"] = np.ascontiguousarray(b1)
    b2 = np.asarray(I["L1_ffn_b2"], np.float32).reshape(NEXP, 8, 128) \
        .transpose(2, 0, 1).reshape(128, NEXP * 8)
    base["b2c"] = np.ascontiguousarray(b2)
    ge = np.asarray(I["L1_ffn_game_emb"], np.float32).reshape(NEXP, 8, 128) \
        .transpose(2, 0, 1).reshape(128, NEXP * 8)
    base["gec"] = np.ascontiguousarray(ge)

    x = np.asarray(I["L0_attn_x"], np.float32)
    in_maps = []
    for ci in range(NCORES):
        m = dict(base)
        m["x_fm"] = np.ascontiguousarray(x[ci].T)
        in_maps.append(m)
    return in_maps



class K:
    def __init__(self, nc, blocks):
        self.nc = nc
        self.p = Prog(nc)
        self.blocks = blocks
        self.inp = {}
        self.c = {}

    def din(self, name, shape, dtype=F32):
        if name not in self.inp:
            self.inp[name] = self.p.dram(name, shape, dtype, kind="ExternalInput")
        return self.inp[name]


def declare_inputs(k):
    k.din("x_fm", [D, T])
    k.din("eyef", [128, 128])
    k.din("pri16", [128, 128])
    k.din("tokid", [128, 16])
    if "l0a" in k.blocks:
        k.din("w_in", [D, 4096]); k.din("w_dt", [DI, DI]); k.din("w_x", [DI, 128])
        k.din("w_out", [DI, D]); k.din("ln_g", [128, NCH]); k.din("ln_b", [128, NCH])
        k.din("conv_b", [128, NCHI]); k.din("b_dt", [128, NCHI])
        k.din("d_vec", [128, NCHI]); k.din("conv_w", [128, NCHI * 4])
        k.din("a_log", [1, 64])
    if "l0f" in k.blocks:
        k.din("wfr", [D, D]); k.din("wfi", [D, D]); k.din("fbias", [128, NCH])
        k.din("tC", [D, D]); k.din("tS", [D, D]); k.din("tCn", [D, D])
    if "l1a" in k.blocks:
        for n in ("wk_t", "wv_t", "wr_t", "wo_t"):
            k.din(n, [D, D])
        for n in ("tdec", "tfirst", "tmk", "tmv", "tmr"):
            k.din(n, [128, NCH])
    if "l1f" in k.blocks:
        k.din("rw", [D, NEXP]); k.din("rb16", [1, 128])
        k.din("moe_w1", [NEXP * D, 4096]); k.din("moe_w2", [NEXP * 4096, D])
        k.din("b1c", [128, NEXP * 32]); k.din("b2c", [128, NEXP * 8])
        k.din("gec", [128, NEXP * 8])


def dslice(tb, apfn):
    return tb.rv(apfn)


def kmaj(dram_tb, c0, c1, p=128):
    return dram_tb.rv(lambda: dram_tb.h.ap()[:, c0:c1]
                      .rearrange("(kc p) n -> p kc n", p=p))


def kmajr(dram_tb, r0, r1, c0, c1, p=128):
    return dram_tb.rv(lambda: dram_tb.h.ap()[r0:r1, c0:c1]
                      .rearrange("(kc p) n -> p kc n", p=p))


def ln_group_stats(k, X, t0, double_ln, tag):
    p, c = k.p, k.c
    R = c["rowscr"]
    s0, s1, s2, s3, s4 = (R[0:1, i * 512:(i + 1) * 512] for i in range(5))
    for ch in range(NCH):
        xsl = X[:, ch * T + t0: ch * T + t0 + 512]
        p.copy(c["xg_bf"][:, ch * 512:(ch + 1) * 512], xsl, tag=tag + "c")
        p.act(c["sg_bf"][:, ch * 512:(ch + 1) * 512],
              c["xg_bf"][:, ch * 512:(ch + 1) * 512], AF.Square, tag=tag + "q")
    for ch in range(NCH):
        p.mm(c["psU"][0:1, 0:512], c["ones_bf"][:, 0:1],
             c["xg_bf"][:, ch * 512:(ch + 1) * 512],
             start=(ch == 0), stop=(ch == NCH - 1), tag=tag + "mS")
    for ch in range(NCH):
        p.mm(c["psE"][0:1, 0:512], c["ones_bf"][:, 0:1],
             c["sg_bf"][:, ch * 512:(ch + 1) * 512],
             start=(ch == 0), stop=(ch == NCH - 1), tag=tag + "mQ")
    p.ts(s0, c["psU"][0:1, 0:512], 1.0 / D, tag=tag + "mean")
    p.tt(s1, s0, s0, ALU.mult, tag=tag + "m2")
    p.stt(s1, c["psE"][0:1, 0:512], 1.0 / D, s1, ALU.mult, ALU.subtract,
          tag=tag + "var")
    p.act(s2, s1, AF.Sqrt, bias=EPS, tag=tag + "sq")
    p.recip(s2, s2, tag=tag + "rc")
    if not double_ln:
        return s0, s2, None
    p.tt(s3, s2, s2, ALU.mult, tag=tag + "r1sq")
    p.tt(s3, s3, s1, ALU.mult, tag=tag + "w")
    p.act(s3, s3, AF.Sqrt, bias=EPS, tag=tag + "sq2")
    p.recip(s3, s3, tag=tag + "rc2")
    p.tt(s4, s3, s2, ALU.mult, tag=tag + "r12")
    return s0, s2, s4


def bc512(k, row_ref, dst_sbuf, tag):
    p, c = k.p, k.c
    p.mm(c["psE"][:, 0:512], c["ones_f1"].full, row_ref, start=True, stop=True,
         tag=tag + "b")
    p.copy(dst_sbuf, c["psE"][:, 0:512], tag=tag + "c")


def build_program(blocks):
    nc = bass.Bass("TRN2", target_bir_lowering=False, debug=False)
    k = K(nc, blocks)
    declare_inputs(k)
    p = k.p
    final = "l1f" in blocks
    out_t = p.dram("y_out", [T, D], F32, kind="ExternalOutput") if final \
        else p.dram("y_out_fm", [D, T], F32, kind="ExternalOutput")

    c = k.c
    c["X"] = p.sbuf("X", (128, NCH * T), F32)
    c["ones_bf"] = p.sbuf("ones_bf", (128, 1), BF16)
    c["ones_f1"] = p.sbuf("ones_f1", (1, 128), F32)
    c["rowscr"] = p.sbuf("rowscr", (1, 5 * 512), F32)
    c["psA"] = p.psum("psA", (128, 512), F32)
    c["psB"] = p.psum("psB", (128, 512), F32)
    c["psC"] = p.psum("psC", (128, 512), F32)
    c["psD"] = p.psum("psD", (128, 512), F32)
    c["psE"] = p.psum("psE", (128, 512), F32)
    c["psU"] = p.psum("psU", (128, 512), F32)
    c["xg_bf"] = p.sbuf("xg_bf", (128, NCH * 512), BF16)
    c["sg_bf"] = p.sbuf("sg_bf", (128, NCH * 512), BF16)
    c["mbc"] = p.sbuf("mbc", (128, 512), F32)
    c["rbc"] = p.sbuf("rbc", (128, 512), F32)
    c["r12bc"] = p.sbuf("r12bc", (128, 512), F32)
    c["eyef"] = p.sbuf("eyef_t", (128, 128), F32)
    X = c["X"]

    p.memset(c["ones_bf"].full, 1.0)
    p.memset(c["ones_f1"].full, 1.0)
    p.dma("sync", c["eyef"].full, k.inp["eyef"].full, tag="ld_eye")
    for ch in range(NCH):
        p.dma("sync", X[:, ch * T:(ch + 1) * T],
              k.inp["x_fm"][ch * 128:(ch + 1) * 128, :], tag="ld_x")

    if "l0a" in blocks:
        emit_l0_attn(k)
    if "l0f" in blocks:
        emit_l0_ffn(k)
    if "l1a" in blocks:
        emit_l1_attn(k)
    if "l1f" in blocks:
        emit_l1_ffn(k, out_t)
    else:
        for ch in range(NCH):
            p.dma("sync", out_t[ch * 128:(ch + 1) * 128, :],
                  X[:, ch * T:(ch + 1) * T], tag="st_x")
    p.lower_and_emit()
    return nc



def emit_l0_attn(k):
    p, c = k.p, k.c
    X = c["X"]

    ln_g = p.sbuf("ln_g_t", (128, NCH), F32)
    ln_b = p.sbuf("ln_b_t", (128, NCH), F32)
    conv_b = p.sbuf("conv_b_t", (128, NCHI), F32)
    conv_w = p.sbuf("conv_w_t", (128, NCHI * 4), F32)
    b_dt = p.sbuf("b_dt_t", (128, NCHI), F32)
    d_vec = p.sbuf("d_vec_t", (128, NCHI), F32)
    a_log = p.sbuf("a_log_t", (1, 64), F32)
    for n, t in [("ln_g", ln_g), ("ln_b", ln_b), ("conv_b", conv_b),
                 ("conv_w", conv_w), ("b_dt", b_dt), ("d_vec", d_vec),
                 ("a_log", a_log)]:
        p.dma("sync", t.full, k.inp[n].full, tag="ld_" + n)
    w_x_bf = p.sbuf("w_x_bf", (128, NCHI * 128), BF16)
    p.dma("pool", w_x_bf.full, kmaj(k.inp["w_x"], 0, 128), tag="ld_wx")

    a_row = p.sbuf("a_row", (1, 64), F32)
    p.act(a_row.full, a_log.full, AF.Exp, tag="a_exp", fs=True)
    p.ts(a_row.full, a_row.full, -1.0, tag="a_neg", fs=True)

    win_g = [p.sbuf(f"win_g{i}", (128, NCH * 512), BF16) for i in range(2)]
    wdt_g = [p.sbuf("wdt_g0", (128, NCHI * 512), BF16)]
    wout_g = [p.sbuf("wout_g0", (128, NCHI * 128), BF16)]

    xn_tt = p.sbuf("xn_tt", (128, NCH * TT), BF16)
    ln1o = p.sbuf("ln1o", (128, TT), F32)
    fscr = p.sbuf("fscr", (128, TT), F32)
    xp = p.sbuf("xp", (128, NCHI * (TT + 3)), BF16)
    res = p.sbuf("res_t", (128, NCHI * TT), BF16)
    xc = p.sbuf("xc", (128, NCHI * TT), BF16)
    sp = fscr
    sp2 = ln1o
    dmacc = p.sbuf("dmacc", (1, TT), F32)
    ad = p.sbuf("ad", (64, TT), F32)
    bst = c["r12bc"]
    hsc = p.sbuf("hsc", (64, TT), F32)
    hc_carry = p.sbuf("hc_carry", (64, 1), F32)
    y_row = p.sbuf("y_row", (1, TT), F32)
    silu_r = c["rbc"]
    ybc = c["mbc"]
    p.memset(hc_carry.full, 0.0)

    for tt in range(NTT):
        t0 = tt * TT
        wb = tt % 2
        m_row, r1_row, r12_row = ln_group_stats(k, X, t0, True, f"ln1_{tt}")
        bc512(k, m_row, c["mbc"].full, "bm")
        bc512(k, r1_row, c["rbc"].full, "br")
        bc512(k, r12_row, c["r12bc"].full, "b12")
        for ch in range(NCH):
            xsl = X[:, ch * T + t0: ch * T + t0 + TT]
            p.tt(ln1o.full, xsl, c["mbc"].full, ALU.subtract, tag="l1sub")
            p.tt(fscr.full, ln1o.full, c["r12bc"].full, ALU.mult, tag="l1r12")
            p.act(xn_tt[:, ch * TT:(ch + 1) * TT], fscr.full, AF.Identity,
                  bias=ln_b[:, ch:ch + 1], scale=ln_g[:, ch:ch + 1], tag="l1aff")
            p.tt(ln1o.full, ln1o.full, c["rbc"].full, ALU.mult, tag="l1r1")
            p.tt(xsl, xsl, ln1o.full, ALU.add, tag="l1add")
        for grp in range(8):
            n0 = grp * 512
            p.dma("pool", win_g[grp % 2].full, kmaj(k.inp["w_in"], n0, n0 + 512),
                  tag="ld_win")
            for ncc in range(4):
                ps = c["psA"] if ncc % 2 == 0 else c["psB"]
                for kc in range(NCH):
                    p.mm(ps[:, 0:512],
                         win_g[grp % 2][:, kc * 512 + ncc * 128: kc * 512 + ncc * 128 + 128],
                         xn_tt[:, kc * TT:(kc + 1) * TT],
                         start=(kc == 0), stop=(kc == NCH - 1), tag="mm_win")
                ic = grp * 4 + ncc
                if ic < NCHI:
                    p.copy(xp[:, ic * (TT + 3) + 3: (ic + 1) * (TT + 3)],
                           ps[:, 0:512], tag="ev_xp")
                else:
                    p.copy(res[:, (ic - NCHI) * TT:(ic - NCHI + 1) * TT],
                           ps[:, 0:512], tag="ev_res")
        for ic in range(NCHI):
            b0 = ic * (TT + 3)
            if tt == 0:
                p.memset(xp[:, b0:b0 + 3], 0.0, tag="xp0", fs=True)
            p.act(sp.full, xp[:, b0:b0 + TT], AF.Copy,
                  scale=conv_w[:, ic * 4 + 0:ic * 4 + 1], tag="cv0")
            for j in (1, 2, 3):
                p.stt(sp.full, xp[:, b0 + j:b0 + j + TT],
                      conv_w[:, ic * 4 + j:ic * 4 + j + 1], sp.full,
                      ALU.mult, ALU.add, tag=f"cv{j}")
            p.act(xc[:, ic * TT:(ic + 1) * TT], sp.full, AF.Silu,
                  bias=conv_b[:, ic:ic + 1], tag="cv_silu")
        if tt < NTT - 1:
            for ic in range(NCHI):
                b0 = ic * (TT + 3)
                p.copy(xp[:, b0:b0 + 3], xp[:, b0 + TT:b0 + TT + 3], tag="xp_c", fs=True)
        p.memset(dmacc.full, 0.0, tag="dm0", fs=True)
        for grp in range(4):
            p.dma("pool", wdt_g[0].full,
                  kmaj(k.inp["w_dt"], grp * 512, (grp + 1) * 512), tag="ld_wdt")
            for ncc in range(4):
                ic = grp * 4 + ncc
                ps = c["psC"] if ncc % 2 == 0 else c["psD"]
                for kc in range(NCHI):
                    p.mm(ps[:, 0:512],
                         wdt_g[0][:, kc * 512 + ncc * 128: kc * 512 + ncc * 128 + 128],
                         xc[:, kc * TT:(kc + 1) * TT],
                         start=(kc == 0), stop=(kc == NCHI - 1), tag="mm_wdt")
                p.act(sp.full, ps[:, 0:512], AF.Abs, bias=b_dt[:, ic:ic + 1],
                      tag="spA")
                p.act(sp.full, sp.full, AF.Exp, scale=-1.0, tag="spE")
                p.act(sp.full, sp.full, AF.Ln, bias=1.0, tag="spL")
                p.act(sp2.full, ps[:, 0:512], AF.Relu, bias=b_dt[:, ic:ic + 1],
                      tag="spR")
                p.tt(sp2.full, sp2.full, sp.full, ALU.add, tag="spS")
                p.copy(c["sg_bf"][:, 0:512], sp2.full, tag="spB")
                p.mm(c["psU"][0:1, 0:512], c["ones_bf"][:, 0:1],
                     c["sg_bf"][:, 0:512], start=True, stop=True, tag="mm_dm")
                p.tt(dmacc.full, dmacc.full, c["psU"][0:1, 0:512], ALU.add,
                     tag="dm_a", fs=True)
        p.ts(dmacc.full, dmacc.full, 1.0 / DI, tag="dm_s", fs=True)
        p.mm(c["psU"][0:64, 0:512], a_row.full, dmacc.full, start=True, stop=True,
             tag="mm_ad")
        p.act(ad.full, c["psU"][0:64, 0:512], AF.Exp, tag="ad_e")
        bstB = bst[0:64, 0:512]
        bstC = c["rbc"][0:64, 0:512]
        for kc in range(NCHI):
            p.mm(c["psA"][0:64, 0:512], w_x_bf[:, kc * 128:kc * 128 + 64],
                 xc[:, kc * TT:(kc + 1) * TT],
                 start=(kc == 0), stop=(kc == NCHI - 1), tag="mm_wxB")
        p.copy(bstB, c["psA"][0:64, 0:512], tag="ev_bcB")
        for kc in range(NCHI):
            p.mm(c["psB"][0:64, 0:512], w_x_bf[:, kc * 128 + 64:(kc + 1) * 128],
                 xc[:, kc * TT:(kc + 1) * TT],
                 start=(kc == 0), stop=(kc == NCHI - 1), tag="mm_wxC")
        p.copy(bstC, c["psB"][0:64, 0:512], tag="ev_bcC")
        p.scan(hsc.full, ad.full, bstB, hc_carry[:, 0:1], ALU.mult,
               ALU.add, tag="scan_h")
        p.copy(hc_carry.full, hsc[:, TT - 1:TT], tag="h_cr", fs=True)
        p.tt(hsc.full, hsc.full, bstC, ALU.mult, tag="h_c")
        p.copy(c["sg_bf"][0:64, 0:512], hsc.full, tag="hc_b")
        p.mm(c["psU"][0:1, 0:512], c["ones_bf"][0:64, 0:1],
             c["sg_bf"][0:64, 0:512], start=True, stop=True, tag="mm_yr")
        p.copy(y_row.full, c["psU"][0:1, 0:512], tag="ev_yr", fs=True)
        p.mm(c["psE"][:, 0:512], c["ones_f1"].full, y_row.full, start=True,
             stop=True, tag="bc_y")
        p.copy(ybc.full, c["psE"][:, 0:512], tag="bc_yc")
        for ic in range(NCHI):
            p.act(silu_r.full, res[:, ic * TT:(ic + 1) * TT], AF.Silu, tag="y_si")
            p.stt(sp.full, xc[:, ic * TT:(ic + 1) * TT], d_vec[:, ic:ic + 1],
                  ybc.full, ALU.mult, ALU.add, tag="y_dx")
            p.tt(xc[:, ic * TT:(ic + 1) * TT], sp.full, silu_r.full, ALU.mult,
                 tag="y_m")
        for oc in range(NCH):
            p.dma("pool", wout_g[0].full,
                  kmaj(k.inp["w_out"], oc * 128, (oc + 1) * 128), tag="ld_wo")
            ps = c["psB"] if oc % 2 == 0 else c["psC"]
            for kc in range(NCHI):
                p.mm(ps[:, 0:512], wout_g[0][:, kc * 128:(kc + 1) * 128],
                     xc[:, kc * TT:(kc + 1) * TT],
                     start=(kc == 0), stop=(kc == NCHI - 1), tag="mm_wo")
            xsl = X[:, oc * T + t0: oc * T + t0 + TT]
            p.tt(xsl, xsl, ps[:, 0:512], ALU.add, tag="res_add")


def emit_l0_ffn(k):
    p, c = k.p, k.c
    X = c["X"]
    NI = 8
    Cb = p.sbuf("Cb", (128, NI * 1024), BF16)
    Sb = p.sbuf("Sb", (128, NI * 1024), BF16)
    wrb = p.sbuf("wrb", (128, NI * 1024), BF16)
    wib = p.sbuf("wib", (128, NI * 1024), BF16)
    P1T = p.sbuf("P1T", (128, NI * 1024), BF16)
    P2TN = p.sbuf("P2TN", (128, NI * 1024), BF16)
    fb = p.sbuf("fb_t", (128, NCH), F32)
    xn2 = p.sbuf("xn2", (128, NCH * 512), BF16)
    f2scr = p.sbuf("f2scr", (128, 512), F32)
    p.dma("pool", Cb.full, kmaj(k.inp["tC"], 0, 1024), tag="ld_C")
    p.dma("pool", Sb.full, kmaj(k.inp["tS"], 0, 1024), tag="ld_S")
    p.dma("pool", wrb.full, kmaj(k.inp["wfr"], 0, 1024), tag="ld_wr")
    p.dma("pool", wib.full, kmaj(k.inp["wfi"], 0, 1024), tag="ld_wi")
    p.dma("sync", fb.full, k.inp["fbias"].full, tag="ld_fb")

    def st1(dst, lA, rA, lB, rB, scale, tag):
        for oc in range(NI):
            for ih in range(2):
                ps = c["psA"] if ih == 0 else c["psB"]
                for d in range(NI):
                    p.mm(ps[:, 0:512],
                         lA[:, d * 1024 + oc * 128: d * 1024 + oc * 128 + 128],
                         rA[:, d * 1024 + ih * 512: d * 1024 + ih * 512 + 512],
                         start=(d == 0), stop=False, tag=tag + "a")
                for d in range(NI):
                    p.mm(ps[:, 0:512],
                         lB[:, d * 1024 + oc * 128: d * 1024 + oc * 128 + 128],
                         rB[:, d * 1024 + ih * 512: d * 1024 + ih * 512 + 512],
                         start=False, stop=(d == NI - 1), tag=tag + "b")
                p.act(dst[:, oc * 1024 + ih * 512: oc * 1024 + ih * 512 + 512],
                      ps[:, 0:512], AF.Copy, scale=scale, tag=tag + "ev")

    st1(P1T, wrb, Cb, wib, Sb, 1.0, "p1")
    for d in range(NI):
        p.act(wib[:, d * 1024:(d + 1) * 1024], wib[:, d * 1024:(d + 1) * 1024],
              AF.Copy, scale=-1.0, tag="wineg")
    st1(P2TN, wrb, Sb, wib, Cb, 1.0, "p2")
    Mt = wrb
    st1(Mt, P1T, Cb, P2TN, Sb, 1.0 / D, "m2")

    for g in range(4):
        t0 = g * 512
        m_row, r1_row, _ = ln_group_stats(k, X, t0, False, f"ln2_{g}")
        bc512(k, m_row, c["mbc"].full, "f2m")
        bc512(k, r1_row, c["rbc"].full, "f2r")
        for ch in range(NCH):
            xsl = X[:, ch * T + t0: ch * T + t0 + 512]
            p.tt(f2scr.full, xsl, c["mbc"].full, ALU.subtract, tag="f2sub")
            p.tt(xn2[:, ch * 512:(ch + 1) * 512], f2scr.full, c["rbc"].full,
                 ALU.mult, tag="f2mul")
        for jc in range(NCH):
            ps = c["psC"] if jc % 2 == 0 else c["psD"]
            for icn in range(NI):
                p.mm(ps[:, 0:512],
                     Mt[:, icn * 1024 + jc * 128: icn * 1024 + jc * 128 + 128],
                     xn2[:, icn * 512:(icn + 1) * 512],
                     start=(icn == 0), stop=(icn == NI - 1), tag="mm_y2")
            xsl = X[:, jc * T + t0: jc * T + t0 + 512]
            p.stt(xsl, ps[:, 0:512], fb[:, jc:jc + 1], xsl, ALU.add, ALU.add,
                  tag="f2res")


def emit_l1_attn(k):
    p, c = k.p, k.c
    X = c["X"]
    HL = 1024
    prm = {}
    for n in ("tdec", "tfirst", "tmk", "tmv", "tmr"):
        prm[n] = p.sbuf(n + "_t", (128, NCH), F32)
        p.dma("sync", prm[n].full, k.inp[n].full, tag="ld_" + n)
    e1 = p.sbuf("e1_t", (128, NCH), F32)
    wn = p.sbuf("wn_t", (128, NCH), F32)
    ew = p.sbuf("ew_t", (128, NCH), F32)
    eu = p.sbuf("eu_t", (128, NCH), F32)
    p.act(e1.full, prm["tdec"].full, AF.Exp, tag="wk_e1", fs=True)
    p.ts(wn.full, e1.full, -1.0, tag="wk_wn", fs=True)
    p.act(ew.full, e1.full, AF.Exp, scale=-1.0, tag="wk_ew", fs=True)
    p.act(eu.full, prm["tfirst"].full, AF.Exp, tag="wk_eu", fs=True)

    xn3h = p.sbuf("xn3h", (128, NCH * (HL + 1)), BF16)
    ry_bf = xn3h
    xk_bf = p.sbuf("xk_bf", (128, NCH * HL), BF16)
    xv_bf = p.sbuf("xv_bf", (128, NCH * HL), BF16)
    xr_bf = p.sbuf("xr_bf", (128, NCH * HL), BF16)
    xcar = p.sbuf("xcar", (128, NCH), BF16)
    kscan = p.sbuf("kscan", (128, HL + 1), F32)
    ket = p.sbuf("ket", (128, HL + 1), F32)
    kvet = p.sbuf("kvet", (128, HL + 1), F32)
    ewt = p.sbuf("ewt", (128, HL + 1), F32)
    wlt = p.sbuf("wlt", (128, HL + 1), F32)
    asc = p.sbuf("asc", (128, HL + 1), F32)
    bsc = p.sbuf("bsc", (128, HL + 1), F32)
    psc = p.sbuf("psc", (128, HL + 1), F32)
    onesl = p.sbuf("onesl", (128, HL + 1), F32)
    mscr = p.sbuf("mscr", (128, HL), F32)
    rsig = mscr
    acar = p.sbuf("acar", (128, NCH), F32)
    bcar = p.sbuf("bcar", (128, NCH), F32)
    pcar = p.sbuf("pcar", (128, NCH), F32)
    wbuf = [p.sbuf(f"l1w{i}", (128, NCH * 128), BF16) for i in range(2)]
    p.memset(onesl.full, 1.0)
    p.memset(acar.full, 0.0)
    p.memset(bcar.full, 0.0)
    p.memset(pcar.full, -1e30)

    nwl = [0]

    def wload(name, cc):
        b = wbuf[nwl[0] % 2]
        nwl[0] += 1
        p.dma("pool", b.full, kmaj(k.inp[name], cc * 128, (cc + 1) * 128),
              tag="ld_" + name)
        return b

    for h in range(2):
        t0 = h * HL
        for ch in range(NCH):
            b0 = ch * (HL + 1)
            if h == 0:
                p.memset(xn3h[:, b0:b0 + 1], 0.0, tag="x3z", fs=True)
            else:
                p.copy(xn3h[:, b0:b0 + 1], xcar[:, ch:ch + 1], tag="x3c", fs=True)
        for g in range(2):
            m_row, r1_row, _ = ln_group_stats(k, X, t0 + g * 512, False,
                                              f"ln3_{h}{g}")
            bc512(k, m_row, c["mbc"].full, "l3m")
            bc512(k, r1_row, c["rbc"].full, "l3r")
            for ch in range(NCH):
                xsl = X[:, ch * T + t0 + g * 512: ch * T + t0 + g * 512 + 512]
                p.tt(mscr[:, 0:512], xsl, c["mbc"].full, ALU.subtract, tag="l3sub")
                p.tt(xn3h[:, ch * (HL + 1) + 1 + g * 512:
                          ch * (HL + 1) + 1 + g * 512 + 512],
                     mscr[:, 0:512], c["rbc"].full, ALU.mult, tag="l3mul")
        for ch in range(NCH):
            p.copy(xcar[:, ch:ch + 1], xn3h[:, ch * (HL + 1) + HL:
                                            ch * (HL + 1) + HL + 1], tag="x3s", fs=True)
        for ch in range(NCH):
            b0 = ch * (HL + 1)
            xv_ = xn3h[:, b0 + 1:b0 + 1 + HL]
            xx_ = xn3h[:, b0:b0 + HL]
            p.tt(mscr.full, xv_, xx_, ALU.subtract, tag="mixd")
            for nm, dst in (("tmk", xk_bf), ("tmv", xv_bf), ("tmr", xr_bf)):
                p.stt(dst[:, ch * HL:(ch + 1) * HL], mscr.full,
                      prm[nm][:, ch:ch + 1], xx_, ALU.mult, ALU.add, tag="mix" + nm)
        for cc in range(NCH):
            ecol = lambda t: t[:, cc:cc + 1]
            wk = wload("wk_t", cc)
            for g in range(2):
                ps = c["psA"] if g == 0 else c["psB"]
                for kc in range(NCH):
                    p.mm(ps[:, 0:512], wk[:, kc * 128:(kc + 1) * 128],
                         xk_bf[:, kc * HL + g * 512: kc * HL + g * 512 + 512],
                         start=(kc == 0), stop=(kc == NCH - 1), tag="mm_k")
                p.copy(kscan[:, 1 + g * 512:1 + g * 512 + 512], ps[:, 0:512],
                       tag="ev_k")
            wv = wload("wv_t", cc)
            p.memset(ket[:, 0:1], 0.0, tag="ke0")
            p.act(ket[:, 1:HL + 1], kscan[:, 1:HL + 1], AF.Exp, tag="ke_e")
            p.memset(kvet[:, 0:1], 0.0, tag="kve0")
            for g in range(2):
                ps = c["psC"] if g == 0 else c["psD"]
                for kc in range(NCH):
                    p.mm(ps[:, 0:512], wv[:, kc * 128:(kc + 1) * 128],
                         xv_bf[:, kc * HL + g * 512: kc * HL + g * 512 + 512],
                         start=(kc == 0), stop=(kc == NCH - 1), tag="mm_v")
                p.tt(kvet[:, 1 + g * 512:1 + g * 512 + 512],
                     ket[:, 1 + g * 512:1 + g * 512 + 512], ps[:, 0:512],
                     ALU.mult, tag="kve_m")
            wr = wload("wr_t", cc)
            for g in range(2):
                ps = c["psA"] if g == 0 else c["psB"]
                for kc in range(NCH):
                    p.mm(ps[:, 0:512], wr[:, kc * 128:(kc + 1) * 128],
                         xr_bf[:, kc * HL + g * 512: kc * HL + g * 512 + 512],
                         start=(kc == 0), stop=(kc == NCH - 1), tag="mm_r")
                p.act(rsig[:, g * 512:g * 512 + 512], ps[:, 0:512], AF.Sigmoid,
                      tag="r_sig")
            p.memset(ewt[:, 0:1], 1.0, tag="ew1")
            p.act(ewt[:, 1:HL + 1], onesl[:, 1:HL + 1], AF.Copy, scale=ecol(ew),
                  tag="ew_b")
            p.memset(wlt[:, 0:1], 0.0, tag="wl0")
            p.act(wlt[:, 1:HL + 1], onesl[:, 1:HL + 1], AF.Copy, scale=ecol(wn),
                  tag="wl_b")
            p.memset(kscan[:, 0:1], -3e38, tag="ks0")
            p.scan(asc.full, ewt.full, kvet.full, ecol(acar), ALU.mult, ALU.add,
                   tag="sc_a")
            p.scan(bsc.full, ewt.full, ket.full, ecol(bcar), ALU.mult, ALU.add,
                   tag="sc_b")
            p.scan(psc.full, wlt.full, kscan.full, ecol(pcar), ALU.add, ALU.max,
                   tag="sc_p")
            p.copy(acar[:, cc:cc + 1], asc[:, HL:HL + 1], tag="cr_a", fs=True)
            p.copy(bcar[:, cc:cc + 1], bsc[:, HL:HL + 1], tag="cr_b", fs=True)
            p.copy(pcar[:, cc:cc + 1], psc[:, HL:HL + 1], tag="cr_p", fs=True)
            aev = asc[:, 0:HL]
            bev = bsc[:, 0:HL]
            pev = psc[:, 0:HL]
            p.stt(aev, aev, ecol(eu), kvet[:, 1:HL + 1], ALU.mult, ALU.add,
                  tag="num")
            p.stt(bev, bev, ecol(eu), ket[:, 1:HL + 1], ALU.mult, ALU.add,
                  tag="den")
            p.stt(pev, pev, ecol(prm["tfirst"]), kscan[:, 1:HL + 1], ALU.add,
                  ALU.max, tag="pt")
            p.act(pev, pev, AF.Exp, tag="ep")
            p.stt(pev, pev, 1e-6, bev, ALU.mult, ALU.add, tag="den2")
            p.recip(pev, pev, tag="rcp")
            p.tt(aev, aev, pev, ALU.mult, tag="y")
            p.tt(ry_bf[:, cc * HL:(cc + 1) * HL], aev, rsig.full, ALU.mult,
                 tag="ry")
        for oc in range(NCH):
            wo = wload("wo_t", oc)
            for g in range(2):
                ps = c["psC"] if g == 0 else c["psD"]
                for kc in range(NCH):
                    p.mm(ps[:, 0:512], wo[:, kc * 128:(kc + 1) * 128],
                         ry_bf[:, kc * HL + g * 512: kc * HL + g * 512 + 512],
                         start=(kc == 0), stop=(kc == NCH - 1), tag="mm_wo1")
                xsl = X[:, oc * T + t0 + g * 512: oc * T + t0 + g * 512 + 512]
                p.tt(xsl, xsl, ps[:, 0:512], ALU.add, tag="res1_add")


def emit_l1_ffn(k, out_t):
    p, c = k.p, k.c
    X = c["X"]
    NI32 = 32

    rw_t = p.sbuf("rw_t", (128, NCH * NEXP), F32)
    for kc in range(NCH):
        p.dma("sync", rw_t[:, kc * NEXP:(kc + 1) * NEXP],
              k.inp["rw"][kc * 128:(kc + 1) * 128, :], tag="ld_rw")
    rb_sb = p.sbuf("rb_sb", (1, 128), F32)
    p.dma("sync", rb_sb.full, k.inp["rb16"].full, tag="ld_rb")
    pri = p.sbuf("pri_t", (128, 128), F32)
    p.dma("sync", pri.full, k.inp["pri16"].full, tag="ld_pri")
    tokid = p.sbuf("tokid_t", (128, 16), F32)
    p.dma("sync", tokid.full, k.inp["tokid"].full, tag="ld_tok")
    for n in ("b1c", "b2c", "gec"):
        k.c[n] = p.sbuf(n + "_t", (128, k.inp[n].shape[1]), F32)
        p.dma("sync", k.c[n].full, k.inp[n].full, tag="ld_" + n)
    gec_s = k.c["gec"]
    p.ts(gec_s.full, gec_s.full, 0.1, tag="gec_s", fs=True)
    eyeb = p.sbuf("eyeb", (128, 128), BF16)
    p.copy(eyeb.full, c["eyef"].full, tag="eyeb")

    xn4_bf = p.sbuf("xn4_bf", (128, NCH * 512), BF16)
    x4scr = p.sbuf("x4scr", (128, 512), F32)
    LGs = p.sbuf("LGs", (128, 128), F32)
    p.memset(LGs.full, 0.0, tag="lg0")
    xn4_tm = p.dram("xn4_tm", [T, D], BF16)
    y_moe = p.dram("y_moe", [T, D], BF16)
    slot_src = p.dram("slot_src", [NEXP * CAPE, 1], F32)
    pstb = p.psum("pstb", (128, 128), BF16)

    x4t_sb = p.sbuf("x4t_sb", (128, D), BF16)
    for g in range(4):
        t0 = g * 512
        m_row, r1_row, _ = ln_group_stats(k, X, t0, False, f"ln4_{g}")
        bc512(k, m_row, c["mbc"].full, "l4m")
        bc512(k, r1_row, c["rbc"].full, "l4r")
        for ch in range(NCH):
            xsl = X[:, ch * T + t0: ch * T + t0 + 512]
            p.tt(x4scr.full, xsl, c["mbc"].full, ALU.subtract, tag="l4sub")
            p.tt(x4scr.full, x4scr.full, c["rbc"].full, ALU.mult, tag="l4mul")
            p.copy(xn4_bf[:, ch * 512:(ch + 1) * 512], x4scr.full, tag="l4bf")
            for tsl in range(4):
                ts = g * 4 + tsl
                p.mm(c["psU"][:, 0:NEXP],
                     x4scr[:, tsl * 128:(tsl + 1) * 128],
                     rw_t[:, ch * NEXP:(ch + 1) * NEXP],
                     start=True, stop=True, tag="mm_rt")
                p.tt(LGs[:, ts * NEXP:(ts + 1) * NEXP],
                     LGs[:, ts * NEXP:(ts + 1) * NEXP], c["psU"][:, 0:NEXP],
                     ALU.add, tag="lg_a", fs=True)
        for tbl in range(4):
            tb = g * 4 + tbl
            for ch in range(NCH):
                p.transpose(pstb.full,
                            xn4_bf[:, ch * 512 + tbl * 128:
                                   ch * 512 + (tbl + 1) * 128],
                            eyeb.full, tag="tr_x4")
                p.copy(x4t_sb[:, ch * 128:(ch + 1) * 128], pstb.full, tag="ev_x4")
            p.dma("sync", xn4_tm[tb * 128:(tb + 1) * 128, :], x4t_sb.full,
                  tag="st_x4")
    p.mm(c["psU"][:, 0:128], c["ones_f1"].full, rb_sb.full, start=True, stop=True,
         tag="mm_rb")
    p.tt(LGs.full, LGs.full, c["psU"][:, 0:128], ALU.add, tag="lg_b", fs=True)

    mx = p.sbuf("mx_t", (128, 16), F32)
    eq = p.sbuf("eq_t", (128, 8), F32)
    mxb8 = p.sbuf("mxb8_t", (128, 8), F32)
    ones17 = p.sbuf("ones17", (128, 17), F32)
    p.memset(ones17.full, 1.0, tag="o17", fs=True)
    idxf = p.sbuf("idxf_t", (128, 16), F32)
    for g in range(16):
        lgg = LGs[:, g * NEXP:(g + 1) * NEXP]
        p.red(mx[:, g:g + 1], lgg, ALU.max, tag="am_r", fs=True)
        p.ts(mxb8.full, ones17[:, 0:8], mx[:, g:g + 1], None, ALU.mult,
             tag="am_b", fs=True)
        p.tt(eq.full, lgg, mxb8.full, ALU.subtract, tag="am_d", fs=True)
        p.ts(eq.full, eq.full, 0.0, None, ALU.is_equal, tag="am_e", fs=True)
        p.tt(eq.full, eq.full, pri[:, 0:NEXP], ALU.mult, tag="am_p", fs=True)
        p.red(idxf[:, g:g + 1], eq.full, ALU.max, tag="am_r2", fs=True)
    p.ts(idxf.full, idxf.full, -1.0, 8.0, ALU.mult, ALU.add, tag="am_i", fs=True)

    me = p.sbuf("me_t", (128, 17), F32)
    rsc = p.sbuf("rsc_t", (128, 17), F32)
    tote = p.sbuf("tote", (128, NEXP), F32)
    pofs = p.sbuf("pofs", (128, NEXP), F32)
    dacc = p.sbuf("dacc", (128, 16), F32)
    dscr = p.sbuf("dscr", (128, 16), F32)
    ofl = p.sbuf("ofl", (128, 16), F32)
    rank_e = []
    p.memset(dacc.full, 0.0, tag="d0", fs=True)
    for e_ in range(NEXP):
        p.ts(me[:, 1:17], idxf.full, float(e_), None, ALU.is_equal, tag="me", fs=True)
        p.memset(me[:, 0:1], 0.0, tag="me0", fs=True)
        p.scan(rsc.full, ones17.full, me.full, 0.0, ALU.mult, ALU.add, tag="rk")
        p.copy(tote[:, e_:e_ + 1], rsc[:, 16:17], tag="tot", fs=True)
        re = p.sbuf(f"rke{e_}", (128, 16), F32)
        rank_e.append(re)
        p.copy(re.full, rsc[:, 0:16], tag="rkc", fs=True)
    p.transpose(c["psU"][0:NEXP, 0:128], tote.full, c["eyef"].full, tag="tr_tot")
    totT = p.sbuf("totT", (NEXP, 129), F32)
    p.memset(totT[:, 0:1], 0.0, tag="tT0", fs=True)
    p.copy(totT[0:NEXP, 1:129], c["psU"][0:NEXP, 0:128], tag="tTc", fs=True)
    onesP = p.sbuf("onesP", (NEXP, 129), F32)
    p.memset(onesP.full, 1.0, tag="oP")
    cofT = p.sbuf("cofT", (NEXP, 129), F32)
    p.scan(cofT.full, onesP.full, totT.full, 0.0, ALU.mult, ALU.add, tag="sc_t")
    p.transpose(c["psU"][0:128, 0:NEXP], cofT[0:NEXP, 0:128],
                c["eyef"][0:NEXP, 0:NEXP], tag="tr_cof")
    p.copy(pofs.full, c["psU"][0:128, 0:NEXP], tag="pofs_c", fs=True)
    for e_ in range(NEXP):
        p.ts(dscr.full, rank_e[e_].full, pofs[:, e_:e_ + 1],
             float(CAPE * e_), ALU.add, ALU.add, tag="dst1", fs=True)
        p.ts(ofl.full, dscr.full, float(1 - CAPE * (e_ + 1)), None, ALU.add,
             tag="dst2a", fs=True)
        p.relu(ofl.full, ofl.full, tag="dst2b", fs=True)
        p.stt(dscr.full, ofl.full, 1e6, dscr.full, ALU.mult, ALU.add, tag="dst3", fs=True)
        p.ts(me[:, 1:17], idxf.full, float(e_), None, ALU.is_equal, tag="dst4", fs=True)
        p.tt(dscr.full, dscr.full, me[:, 1:17], ALU.mult, tag="dst5", fs=True)
        p.tt(dacc.full, dacc.full, dscr.full, ALU.add, tag="dst6", fs=True)
    di32 = p.sbuf("di32", (128, 16), I32)
    p.copy(di32.full, dacc.full, tag="dcast", fs=True)

    islot = p.sbuf("islot", (128, 32), F32)
    p.memset(islot.full, 1e9, tag="sl9")
    p.dma("pool", slot_src.rv(lambda: slot_src.h.ap()
                              .rearrange("(a b) o -> a (b o)", a=128)),
          islot.full, tag="sl_init")
    for j in range(16):
        p.iscatter(slot_src, tokid[:, j:j + 1], di32[:, j:j + 1],
                   NEXP * CAPE - 1, tag="sl_sc")
    slotf = p.sbuf("slotf", (128, 32), F32)
    for cc in range(32):
        p.dma("pool", slotf[:, cc:cc + 1],
              slot_src[cc * 128:(cc + 1) * 128, :], tag="sl_rd")
    si32 = p.sbuf("si32", (128, 32), I32)
    p.copy(si32.full, slotf.full, tag="scast", fs=True)

    ymo_sb = p.sbuf("ymo_sb", (128, D), BF16)
    p.memset(ymo_sb.full, 0.0, tag="ym0")
    for tb in range(16):
        p.dma("sync", y_moe[tb * 128:(tb + 1) * 128, :], ymo_sb.full,
              tag="ym_init")

    xg_tm = p.sbuf("xg_tm", (128, 4 * D), BF16)
    xsel = p.sbuf("xsel", (128, NCH * CAPE), BF16)
    h_bf = p.sbuf("h_bf", (128, NI32 * CAPE), BF16)
    w1g = p.sbuf("w1g", (128, NCH * 512), BF16)
    w2g = p.sbuf("w2g", (128, NI32 * 128), BF16)
    ytmp = p.sbuf("ytmp", (128, CAPE), F32)
    yg_tm = p.sbuf("yg_tm", (128, 4 * D), BF16)
    for e_ in range(NEXP):
        for b in range(4):
            p.igather(xg_tm[:, b * D:(b + 1) * D], xn4_tm,
                      si32[:, e_ * 4 + b:e_ * 4 + b + 1], bound=T - 1, tag="gx")
            for ch in range(NCH):
                p.transpose(pstb.full,
                            xg_tm[:, b * D + ch * 128: b * D + (ch + 1) * 128],
                            eyeb.full, tag="tr_xs")
                p.act(xsel[:, ch * CAPE + b * 128: ch * CAPE + (b + 1) * 128],
                      pstb.full, AF.Identity,
                      bias=gec_s[:, e_ * NCH + ch:e_ * NCH + ch + 1], tag="ev_xs")
        for ig in range(8):
            p.dma("pool", w1g.full,
                  kmajr(k.inp["moe_w1"], e_ * D, (e_ + 1) * D,
                        ig * 512, (ig + 1) * 512), tag="ld_w1")
            for ic4 in range(4):
                ic = ig * 4 + ic4
                ps = c["psA"] if ic4 % 2 == 0 else c["psB"]
                for kc in range(NCH):
                    p.mm(ps[:, 0:512],
                         w1g[:, kc * 512 + ic4 * 128: kc * 512 + ic4 * 128 + 128],
                         xsel[:, kc * CAPE:(kc + 1) * CAPE],
                         start=(kc == 0), stop=(kc == NCH - 1), tag="mm_h")
                p.act(h_bf[:, ic * CAPE:(ic + 1) * CAPE], ps[:, 0:512], AF.Gelu,
                      bias=k.c["b1c"][:, e_ * 32 + ic:e_ * 32 + ic + 1],
                      tag="ev_h")
        for oc in range(NCH):
            p.dma("pool", w2g.full,
                  kmajr(k.inp["moe_w2"], e_ * 4096, (e_ + 1) * 4096,
                        oc * 128, (oc + 1) * 128), tag="ld_w2")
            ps = c["psC"] if oc % 2 == 0 else c["psD"]
            for kc in range(NI32):
                p.mm(ps[:, 0:512], w2g[:, kc * 128:(kc + 1) * 128],
                     h_bf[:, kc * CAPE:(kc + 1) * CAPE],
                     start=(kc == 0), stop=(kc == NI32 - 1), tag="mm_y")
            p.act(ytmp.full, ps[:, 0:512], AF.Identity,
                  bias=k.c["b2c"][:, e_ * NCH + oc:e_ * NCH + oc + 1], tag="ev_y")
            for b in range(4):
                p.transpose(c["psU"][0:128, 0:128],
                            ytmp[:, b * 128:(b + 1) * 128], c["eyef"].full,
                            tag="tr_y")
                p.copy(yg_tm[:, b * D + oc * 128: b * D + (oc + 1) * 128],
                       c["psU"][0:128, 0:128], tag="ev_yt")
        for b in range(4):
            p.iscatter(y_moe, yg_tm[:, b * D:(b + 1) * D],
                       si32[:, e_ * 4 + b:e_ * 4 + b + 1], T - 1, tag="sc_y")

    if DEBUG_MOE:
        for nm, t_ in [("dbg_lgs", LGs), ("dbg_idx", idxf), ("dbg_dacc", dacc),
                       ("dbg_slotf", slotf), ("dbg_tote", tote),
                       ("dbg_pofs", pofs), ("dbg_mx", mx)]:
            o = p.dram(nm, list(t_.shape), F32, kind="ExternalOutput")
            p.dma("sync", o.full, t_.full, tag="dbg")

    out_sb = p.sbuf("out_sb", (128, D), F32)
    for tb in range(16):
        p.dma("pool", ymo_sb.full, y_moe[tb * 128:(tb + 1) * 128, :], tag="ld_ym")
        for ch in range(NCH):
            p.transpose(c["psU"][0:128, 0:128],
                        X[:, ch * T + tb * 128: ch * T + (tb + 1) * 128],
                        c["eyef"].full, tag="tr_o")
            p.tt(out_sb[:, ch * 128:(ch + 1) * 128],
                 ymo_sb[:, ch * 128:(ch + 1) * 128], c["psU"][0:128, 0:128],
                 ALU.add, tag="o_add")
        p.dma("sync", out_t[tb * 128:(tb + 1) * 128, :], out_sb.full, tag="st_o")



BLOCK_SETS = [("l0a",), ("l0f",), ("l1a",), ("l1f",)]


def kernel(**inputs):
    in_maps = _prep_inputs(inputs)
    x = None
    for bs in BLOCK_SETS:
        nc = build_program(bs)
        if x is not None:
            for ci in range(NCORES):
                in_maps[ci]["x_fm"] = np.ascontiguousarray(x[ci])
        res = run_bass_kernel_spmd(nc, in_maps, core_ids=list(range(NCORES)))
        if "l1f" in bs:
            return np.stack([np.asarray(r["y_out"]) for r in res.results], 0)
        x = np.stack([np.asarray(r["y_out_fm"]) for r in res.results], 0)
    return np.stack([xi.T for xi in x], 0)


# version 06 (final)
# speedup vs baseline: 1.2178x; 1.2178x over previous
import contextlib
import numpy as np

import concourse.bass as bass
import concourse.mybir as mybir
from concourse.bass_utils import run_bass_kernel_spmd

F32 = mybir.dt.float32
BF16 = mybir.dt.bfloat16
I32 = mybir.dt.int32
ALU = mybir.AluOpType
AX = mybir.AxisListType
AF = mybir.ActivationFunctionType
IOA = bass.IndirectOffsetOnAxis

NCORES = 8
T = 2048
D = 1024
DI = 2048
NCH = D // 128
NCHI = DI // 128
TT = 512
NTT = T // TT
CAPE = 512
NEXP = 8
EPS = 1e-5
DEBUG_MOE = False


ENGS = ("pe", "dve", "act", "pool", "sync")


class _Op:
    __slots__ = ("eng", "emit", "deps", "inc", "milestone", "val_after",
                 "waits", "tag", "fsync", "eseq", "small")

    def __init__(self, eng, emit, deps, inc, tag=""):
        self.eng, self.emit, self.deps, self.inc = eng, emit, deps, inc
        self.milestone = False
        self.val_after = 0
        self.waits = []
        self.tag = tag
        self.fsync = False
        self.small = False


class Ref:
    __slots__ = ("tb", "p0", "p1", "f0", "f1", "apfn")

    def __init__(self, tb, p0, p1, f0, f1, apfn=None):
        self.tb, self.p0, self.p1, self.f0, self.f1 = tb, p0, p1, f0, f1
        self.apfn = apfn

    @property
    def ap(self):
        if self.apfn is not None:
            return self.apfn()
        return self.tb.h.ap()[self.p0:self.p1, self.f0:self.f1]


class TB:

    def __init__(self, prog, handle, shape, name):
        self.prog, self.h, self.shape, self.name = prog, handle, shape, name
        self.accs = []

    def __getitem__(self, key):
        if not isinstance(key, tuple):
            key = (key, slice(None))
        ps, fs = key
        p0, p1, _ = ps.indices(self.shape[0]) if isinstance(ps, slice) else (ps, ps + 1, 1)
        f0, f1, _ = fs.indices(self.shape[1]) if isinstance(fs, slice) else (fs, fs + 1, 1)
        return Ref(self, p0, p1, f0, f1)

    @property
    def full(self):
        return self[:, :]

    def rv(self, apfn):
        return Ref(self, 0, self.shape[0], 0, self.shape[1], apfn)


class CTB:

    def __init__(self, arena, f_off, shape, name=""):
        self.arena, self.f_off, self.shape, self.name = arena, f_off, shape, name

    def __getitem__(self, key):
        if not isinstance(key, tuple):
            key = (key, slice(None))
        ps, fs = key
        p0, p1, _ = ps.indices(self.shape[0]) if isinstance(ps, slice) else (ps, ps + 1, 1)
        f0, f1, _ = fs.indices(self.shape[1]) if isinstance(fs, slice) else (fs, fs + 1, 1)
        return Ref(self.arena, p0, p1, self.f_off + f0, self.f_off + f1)

    @property
    def full(self):
        return self[:, :]


class Prog:
    def __init__(self, nc):
        self.nc = nc
        self.ops = []
        self.by_eng = {e: [] for e in ENGS}
        self.ctx = contextlib.ExitStack()
        self.sems = {}
        self.const_tb = None
        self.const_map = {}
        self.const_n = 0
        self._bregs = {}

    def sbuf(self, name, shape, dtype):
        h = self.ctx.enter_context(self.nc.sbuf_tensor(name, list(shape), dtype))
        return TB(self, h, tuple(shape), name)

    def psum(self, name, shape, dtype=F32):
        h = self.ctx.enter_context(self.nc.psum_tensor(name, list(shape), dtype))
        return TB(self, h, tuple(shape), name)

    def dram(self, name, shape, dtype, kind="Internal"):
        h = self.nc.dram_tensor(name, list(shape), dtype, kind=kind)
        return TB(self, h, tuple(shape), name)

    def _touch(self, ref, is_write, op):
        tb = ref.tb
        deps = []
        if is_write:
            keep = []
            for rec in tb.accs:
                o, w, p0, p1, f0, f1 = rec
                if p0 < ref.p1 and ref.p0 < p1 and f0 < ref.f1 and ref.f0 < f1:
                    deps.append(o)
                    if ref.p0 <= p0 and p1 <= ref.p1 and ref.f0 <= f0 and f1 <= ref.f1:
                        continue
                keep.append(rec)
            tb.accs = keep
        else:
            for rec in tb.accs:
                o, w, p0, p1, f0, f1 = rec
                if w and p0 < ref.p1 and ref.p0 < p1 and f0 < ref.f1 and ref.f0 < f1:
                    deps.append(o)
        tb.accs.append((op, is_write, ref.p0, ref.p1, ref.f0, ref.f1))
        return deps

    def add(self, eng, emit, reads=(), writes=(), deps=(), dma=False, tag="",
            fsync=False):
        op = _Op(eng, emit, list(deps), 16 if dma else 1, tag)
        op.fsync = fsync
        reads = [r.full if isinstance(r, (TB, CTB)) else r for r in reads]
        writes = [w.full if isinstance(w, (TB, CTB)) else w for w in writes]
        if writes and not dma:
            op.small = max(w.f1 - w.f0 for w in writes) < 256
        for r in reads:
            op.deps.extend(self._touch(r, False, op))
        for w in writes:
            op.deps.extend(self._touch(w, True, op))
        op.deps = [d for d in op.deps if d is not op]
        if dma:
            op.milestone = True
        self.ops.append(op)
        self.by_eng[eng].append(op)
        return op

    def dma(self, eng, dst, src, deps=(), tag="dma"):
        def _ap(x):
            return x.h.ap() if isinstance(x, TB) else x.ap
        return self.add(eng, lambda e, d=dst, s=src: e.dma_start(out=_ap(d),
                                                                 in_=_ap(s)),
                        reads=[src], writes=[dst], deps=deps, dma=True, tag=tag)

    def tt(self, out, a, b, op, eng="dve", deps=(), tag="tt", fs=False):
        return self.add(eng, lambda e, o=out, x=a, y=b, al=op:
                        e.tensor_tensor(out=o.ap, in0=x.ap, in1=y.ap, op=al),
                        reads=[a, b], writes=[out], deps=deps, tag=tag, fsync=fs)

    def ts(self, out, a, s1, s2=None, op0=ALU.mult, op1=ALU.bypass, eng="dve",
           deps=(), tag="ts", fs=False):
        if isinstance(s1, Ref) and (out.f1 - out.f0) > 128:
            raise AssertionError(f"AP-scalar tensor_scalar too wide: {tag}")
        rd = [a] + [s for s in (s1, s2) if isinstance(s, Ref)]
        if s2 is None:
            return self.add(eng, lambda e, o=out, x=a,
                            sv=(s1.ap if isinstance(s1, Ref) else s1):
                            e.tensor_single_scalar(out=o.ap, in_=x.ap, scalar=sv,
                                                   op=op0),
                            reads=rd, writes=[out], deps=deps, tag=tag, fsync=fs)
        return self.add(eng, lambda e, o=out, x=a,
                        sv=(s1.ap if isinstance(s1, Ref) else s1),
                        sw=(s2.ap if isinstance(s2, Ref) else s2):
                        e.tensor_scalar(out=o.ap, in0=x.ap, scalar1=sv, scalar2=sw,
                                        op0=op0, op1=op1),
                        reads=rd, writes=[out], deps=deps, tag=tag, fsync=fs)

    def stt(self, out, a, scalar, b, op0, op1, eng="dve", deps=(), tag="stt", fs=False):
        deps = list(deps)
        if isinstance(scalar, float) and scalar not in (0.0, 1.0):
            deps.append(self.constref(scalar)[1])
        rd = [a, b] + ([scalar] if isinstance(scalar, Ref) else [])
        return self.add(eng, lambda e, o=out, x=a, y=b,
                        sv=(scalar.ap if isinstance(scalar, Ref) else scalar):
                        e.scalar_tensor_tensor(out=o.ap, in0=x.ap, scalar=sv,
                                               in1=y.ap, op0=op0, op1=op1),
                        reads=rd, writes=[out], deps=deps, tag=tag, fsync=fs)

    def constref(self, v):
        v = float(v)
        if v not in self.const_map:
            if self.const_tb is None:
                self.const_tb = self.sbuf("constp", (1, 16), F32)
            j = self.const_n
            self.const_n += 1
            r = self.const_tb[0:1, j:j + 1]
            op = self.memset(r, v, tag="const")
            self.nc.const_aps.register(F32, v, r.ap)
            self.const_map[v] = (r, op)
        return self.const_map[v]

    def act(self, out, a, func, bias=0.0, scale=1.0, deps=(), tag="act",
            fs=False):
        deps = list(deps)
        for s in (bias, scale):
            if isinstance(s, float) and s not in (0.0, 1.0):
                deps.append(self.constref(s)[1])
        rd = [a] + [s for s in (bias, scale) if isinstance(s, Ref)]
        return self.add("act", lambda e, o=out, x=a,
                        bv=(bias.ap if isinstance(bias, Ref) else bias),
                        sv=(scale.ap if isinstance(scale, Ref) else scale):
                        e.activation(out=o.ap, in_=x.ap, func=func, bias=bv,
                                     scale=sv),
                        reads=rd, writes=[out], deps=deps, tag=tag, fsync=fs)

    def relu(self, out, a, deps=(), tag="relu", fs=False):
        return self.add("dve", lambda e, o=out, x=a:
                        e.tensor_relu(out=o.ap, in_=x.ap),
                        reads=[a], writes=[out], deps=deps, tag=tag, fsync=fs)

    def copy(self, out, a, eng="dve", deps=(), tag="copy", fs=False):
        return self.add(eng, lambda e, o=out, x=a: e.tensor_copy(out=o.ap, in_=x.ap),
                        reads=[a], writes=[out], deps=deps, tag=tag, fsync=fs)

    def memset(self, out, val, eng="dve", deps=(), tag="memset", fs=False):
        return self.add(eng, lambda e, o=out, v=val: e.memset(o.ap, v),
                        writes=[out], deps=deps, tag=tag, fsync=fs)

    def red(self, out, a, op=ALU.max, deps=(), tag="red", fs=False):
        return self.add("dve", lambda e, o=out, x=a:
                        e.tensor_reduce(out=o.ap, in_=x.ap, axis=AX.X, op=op),
                        reads=[a], writes=[out], deps=deps, tag=tag, fsync=fs)

    def scan(self, out, d0, d1, initial, op0, op1, deps=(), tag="scan"):
        rd = [d0, d1] + ([initial] if isinstance(initial, Ref) else [])
        return self.add("dve", lambda e, o=out, a=d0, b=d1,
                        iv=(initial.ap if isinstance(initial, Ref) else initial):
                        e.tensor_tensor_scan(out=o.ap, data0=a.ap, data1=b.ap,
                                             initial=iv, op0=op0, op1=op1),
                        reads=rd, writes=[out], deps=deps, tag=tag, fsync=True)

    def mm(self, out, lhsT, rhs, start, stop, deps=(), tag="mm"):
        return self.add("pe", lambda e, o=out, l=lhsT, r=rhs, s0=start, s1=stop:
                        e.matmul(o.ap, l.ap, r.ap, start=s0, stop=s1),
                        reads=[lhsT, rhs], writes=[out], deps=deps, tag=tag)

    def transpose(self, out, a, ident, deps=(), tag="tr"):
        return self.add("pe", lambda e, o=out, x=a, i=ident:
                        e.transpose(o.ap, x.ap, i.ap),
                        reads=[a, ident], writes=[out], deps=deps, tag=tag)

    def recip(self, out, a, deps=(), tag="recip", fs=False):
        return self.add("dve", lambda e, o=out, x=a: e.reciprocal(out=o.ap, in_=x.ap),
                        reads=[a], writes=[out], deps=deps, tag=tag, fsync=fs)

    def igather(self, out, table, idx, bound=None, deps=(), tag="ig"):
        def emit(e, o=out, t=table, i=idx, b=bound):
            tap = t.h.ap() if isinstance(t, TB) else t.ap
            if b is None:
                return e.indirect_dma_start(out=o.ap, out_offset=None, in_=tap,
                                            in_offset=IOA(i.ap, 0))
            breg = self.breg(e, b)
            return e.indirect_dma_start(out=o.ap, out_offset=None, in_=tap,
                                        in_offset=IOA(i.ap, 0), bounds_check=breg,
                                        oob_is_err=False)
        return self.add("pool", emit, reads=[table, idx], writes=[out], deps=deps,
                        dma=True, tag=tag)

    def breg(self, e, val):
        key = (id(e), val)
        if key not in self._bregs:
            self._bregs[key] = e.to_reg(val)
        return self._bregs[key]

    def iscatter(self, table, src, idx, bound, deps=(), tag="is"):
        def emit(e, t=table, s=src, i=idx, b=bound):
            breg = self.breg(e, b)
            tap = t.h.ap() if isinstance(t, TB) else t.ap
            return e.indirect_dma_start(out=tap, out_offset=IOA(i.ap, 0), in_=s.ap,
                                        in_offset=None, bounds_check=breg,
                                        oob_is_err=False)
        return self.add("pool", emit, reads=[src, idx], writes=[table], deps=deps,
                        dma=True, tag=tag)

    GAP = 0

    def _needs_edge(self, op, d):
        if d.eng != op.eng or d.inc == 16 or d.fsync or op.fsync:
            return True
        return d.small and (op.eseq - d.eseq) < self.GAP

    def lower_and_emit(self):
        nc = self.nc
        for e in ENGS:
            self.sems[e] = self.ctx.enter_context(nc.semaphore("sem_" + e))
        for e in ENGS:
            for i, op in enumerate(self.by_eng[e]):
                op.eseq = i
        for op in self.ops:
            latest = {}
            for d in op.deps:
                if d is op or not self._needs_edge(op, d):
                    continue
                cur = latest.get(d.eng)
                if cur is None or d.eseq > cur.eseq:
                    latest[d.eng] = d
            op.deps = list(latest.values())
            for d in op.deps:
                d.milestone = True
        vals = {e: 0 for e in ENGS}
        for e in ENGS:
            for op in self.by_eng[e]:
                if op.milestone:
                    vals[e] += op.inc
                    op.val_after = vals[e]
        for e in ENGS:
            seen = {x: 0 for x in ENGS}
            for op in self.by_eng[e]:
                need = {}
                for d in op.deps:
                    need[d.eng] = max(need.get(d.eng, 0), d.val_after)
                for se, v in sorted(need.items()):
                    if v > seen[se]:
                        op.waits.append((se, v))
                        seen[se] = v
        finals = dict(vals)
        block = self.ctx.enter_context(nc.Block())

        def runner(eng_name):
            def run(engine):
                for op in self.by_eng[eng_name]:
                    for se, v in op.waits:
                        engine.wait_ge(self.sems[se], v)
                    inst = op.emit(engine)
                    if op.milestone:
                        inst.then_inc(self.sems[op.eng], op.inc)
                if eng_name == "sync":
                    for de in ("sync", "pool"):
                        if finals[de]:
                            engine.wait_ge(self.sems[de], finals[de])
            return run

        block.tensor(runner("pe"))
        block.vector(runner("dve"))
        block.scalar(runner("act"))
        block.gpsimd(runner("pool"))
        block.sync(runner("sync"))
        self.ctx.close()



def _host_constants():
    j = np.arange(D, dtype=np.int64)
    th = 2.0 * np.pi * ((np.outer(j, j) % D).astype(np.float64)) / D
    tC = np.cos(th).astype(np.float32)
    tS = np.sin(th).astype(np.float32)
    tCn = (-tC).astype(np.float32)
    eye = np.eye(128, dtype=np.float32)
    pri16 = np.tile((8.0 - np.arange(8, dtype=np.float32)), 16)[None, :] \
        .repeat(128, 0).copy()
    tokid = (np.arange(16)[None, :] * 128 + np.arange(128)[:, None]) \
        .astype(np.float32)
    return {"tC": tC, "tS": tS, "tCn": tCn, "eyef": eye, "pri16": pri16,
            "tokid": tokid}


def _chunk_cols(v, nch):
    v = np.asarray(v, np.float32).reshape(-1)
    return np.ascontiguousarray(v.reshape(nch, 128).T)


def _prep_inputs(I):
    A = lambda x: np.ascontiguousarray(np.asarray(x, np.float32))
    base = dict(_host_constants())
    base["w_in"] = A(I["L0_attn_W_in"])
    base["w_dt"] = A(I["L0_attn_W_dt"])
    base["w_x"] = A(I["L0_attn_W_x"])
    base["w_out"] = A(I["L0_attn_W_out"])
    base["ln_g"] = _chunk_cols(I["L0_attn_ln_g"], NCH)
    base["ln_b"] = _chunk_cols(I["L0_attn_ln_b"], NCH)
    base["conv_b"] = _chunk_cols(I["L0_attn_conv_b"], NCHI)
    base["b_dt"] = _chunk_cols(I["L0_attn_b_dt"], NCHI)
    base["d_vec"] = _chunk_cols(I["L0_attn_D"], NCHI)
    cw = np.asarray(I["L0_attn_conv_w"], np.float32) \
        .reshape(NCHI, 128, 4).transpose(1, 0, 2).reshape(128, NCHI * 4)
    base["conv_w"] = np.ascontiguousarray(cw)
    base["a_log"] = A(I["L0_attn_A_log"]).reshape(1, 64)
    base["wfr"] = A(I["L0_ffn_w_freq_real"])
    base["wfi"] = A(I["L0_ffn_w_freq_imag"])
    base["fbias"] = _chunk_cols(I["L0_ffn_bias"], NCH)
    base["wk_t"] = A(np.asarray(I["L1_attn_Wk"]).T)
    base["wv_t"] = A(np.asarray(I["L1_attn_Wv"]).T)
    base["wr_t"] = A(np.asarray(I["L1_attn_Wr"]).T)
    base["wo_t"] = A(np.asarray(I["L1_attn_Wo"]).T)
    base["tdec"] = _chunk_cols(I["L1_attn_time_decay"], NCH)
    base["tfirst"] = _chunk_cols(I["L1_attn_time_first"], NCH)
    base["tmk"] = _chunk_cols(np.asarray(I["L1_attn_time_mix_k"]).reshape(-1), NCH)
    base["tmv"] = _chunk_cols(np.asarray(I["L1_attn_time_mix_v"]).reshape(-1), NCH)
    base["tmr"] = _chunk_cols(np.asarray(I["L1_attn_time_mix_r"]).reshape(-1), NCH)
    base["rw"] = A(I["L1_ffn_router_w"])
    base["rb16"] = np.ascontiguousarray(
        np.tile(np.asarray(I["L1_ffn_router_b"], np.float32), 16)[None, :])
    base["moe_w1"] = A(I["L1_ffn_w1"]).reshape(NEXP * D, 4096)
    base["moe_w2"] = A(I["L1_ffn_w2"]).reshape(NEXP * 4096, D)
    b1 = np.asarray(I["L1_ffn_b1"], np.float32).reshape(NEXP, 32, 128) \
        .transpose(2, 0, 1).reshape(128, NEXP * 32)
    base["b1c"] = np.ascontiguousarray(b1)
    b2 = np.asarray(I["L1_ffn_b2"], np.float32).reshape(NEXP, 8, 128) \
        .transpose(2, 0, 1).reshape(128, NEXP * 8)
    base["b2c"] = np.ascontiguousarray(b2)
    ge = np.asarray(I["L1_ffn_game_emb"], np.float32).reshape(NEXP, 8, 128) \
        .transpose(2, 0, 1).reshape(128, NEXP * 8)
    base["gec"] = np.ascontiguousarray(ge)

    x = np.asarray(I["L0_attn_x"], np.float32)
    in_maps = []
    for ci in range(NCORES):
        m = dict(base)
        m["x_fm"] = np.ascontiguousarray(x[ci].T)
        in_maps.append(m)
    return in_maps



class K:
    def __init__(self, nc, blocks):
        self.nc = nc
        self.p = Prog(nc)
        self.blocks = blocks
        self.inp = {}
        self.c = {}
        self.ab = None
        self.af = None
        self._cb = 0
        self._cf = 0

    def reset_arenas(self):
        self._cb = 0
        self._cf = 0

    def bcarve(self, name, shape, at=None):
        assert shape[0] <= 128
        off = self._cb if at is None else at
        t = CTB(self.ab, off, shape, name)
        if at is None:
            self._cb += shape[1]
        assert off + shape[1] <= self.ab.shape[1], \
            f"bf16 arena overflow at {name}: {off + shape[1]}"
        return t

    def fcarve(self, name, shape, at=None):
        assert shape[0] <= 128
        off = self._cf if at is None else at
        t = CTB(self.af, off, shape, name)
        if at is None:
            self._cf += shape[1]
        assert off + shape[1] <= self.af.shape[1], \
            f"fp32 arena overflow at {name}: {off + shape[1]}"
        return t

    def din(self, name, shape, dtype=F32):
        if name not in self.inp:
            self.inp[name] = self.p.dram(name, shape, dtype, kind="ExternalInput")
        return self.inp[name]


def declare_inputs(k):
    k.din("x_fm", [D, T])
    k.din("eyef", [128, 128])
    k.din("pri16", [128, 128])
    k.din("tokid", [128, 16])
    if "l0a" in k.blocks:
        k.din("w_in", [D, 4096]); k.din("w_dt", [DI, DI]); k.din("w_x", [DI, 128])
        k.din("w_out", [DI, D]); k.din("ln_g", [128, NCH]); k.din("ln_b", [128, NCH])
        k.din("conv_b", [128, NCHI]); k.din("b_dt", [128, NCHI])
        k.din("d_vec", [128, NCHI]); k.din("conv_w", [128, NCHI * 4])
        k.din("a_log", [1, 64])
    if "l0f" in k.blocks:
        k.din("wfr", [D, D]); k.din("wfi", [D, D]); k.din("fbias", [128, NCH])
        k.din("tC", [D, D]); k.din("tS", [D, D]); k.din("tCn", [D, D])
    if "l1a" in k.blocks:
        for n in ("wk_t", "wv_t", "wr_t", "wo_t"):
            k.din(n, [D, D])
        for n in ("tdec", "tfirst", "tmk", "tmv", "tmr"):
            k.din(n, [128, NCH])
    if "l1f" in k.blocks:
        k.din("rw", [D, NEXP]); k.din("rb16", [1, 128])
        k.din("moe_w1", [NEXP * D, 4096]); k.din("moe_w2", [NEXP * 4096, D])
        k.din("b1c", [128, NEXP * 32]); k.din("b2c", [128, NEXP * 8])
        k.din("gec", [128, NEXP * 8])


def dslice(tb, apfn):
    return tb.rv(apfn)


def kmaj(dram_tb, c0, c1, p=128):
    return dram_tb.rv(lambda: dram_tb.h.ap()[:, c0:c1]
                      .rearrange("(kc p) n -> p kc n", p=p))


def kmajr(dram_tb, r0, r1, c0, c1, p=128):
    return dram_tb.rv(lambda: dram_tb.h.ap()[r0:r1, c0:c1]
                      .rearrange("(kc p) n -> p kc n", p=p))


def ln_group_stats(k, X, t0, double_ln, tag):
    p, c = k.p, k.c
    R = c["rowscr"]
    s0, s1, s2 = (R[0:1, i * 512:(i + 1) * 512] for i in range(3))
    if double_ln:
        s3, s4 = c["lnS3"][0:1, 0:512], c["lnS4"][0:1, 0:512]
    for ch in range(NCH):
        xsl = X[:, ch * T + t0: ch * T + t0 + 512]
        p.act(c["xg_bf"][:, ch * 512:(ch + 1) * 512], xsl, AF.Copy,
              tag=tag + "c")
        p.act(c["sg_bf"][:, ch * 512:(ch + 1) * 512],
              c["xg_bf"][:, ch * 512:(ch + 1) * 512], AF.Square, tag=tag + "q")
    for ch in range(NCH):
        p.mm(c["psU"][0:1, 0:512], c["ones_bf"][:, 0:1],
             c["xg_bf"][:, ch * 512:(ch + 1) * 512],
             start=(ch == 0), stop=(ch == NCH - 1), tag=tag + "mS")
    for ch in range(NCH):
        p.mm(c["psE"][0:1, 0:512], c["ones_bf"][:, 0:1],
             c["sg_bf"][:, ch * 512:(ch + 1) * 512],
             start=(ch == 0), stop=(ch == NCH - 1), tag=tag + "mQ")
    p.ts(s0, c["psU"][0:1, 0:512], 1.0 / D, tag=tag + "mean")
    p.tt(s1, s0, s0, ALU.mult, tag=tag + "m2")
    p.stt(s1, c["psE"][0:1, 0:512], 1.0 / D, s1, ALU.mult, ALU.subtract,
          tag=tag + "var")
    p.act(s2, s1, AF.Sqrt, bias=EPS, tag=tag + "sq")
    p.recip(s2, s2, tag=tag + "rc")
    if not double_ln:
        return s0, s2, None
    p.tt(s3, s2, s2, ALU.mult, tag=tag + "r1sq")
    p.tt(s3, s3, s1, ALU.mult, tag=tag + "w")
    p.act(s3, s3, AF.Sqrt, bias=EPS, tag=tag + "sq2")
    p.recip(s3, s3, tag=tag + "rc2")
    p.tt(s4, s3, s2, ALU.mult, tag=tag + "r12")
    return s0, s2, s4


def bc512(k, row_ref, dst_sbuf, tag):
    p, c = k.p, k.c
    p.mm(c["psE"][:, 0:512], c["ones_f1"].full, row_ref, start=True, stop=True,
         tag=tag + "b")
    p.copy(dst_sbuf, c["psE"][:, 0:512], tag=tag + "c")


def build_program(blocks):
    nc = bass.Bass("TRN2", target_bir_lowering=False, debug=False)
    k = K(nc, blocks)
    declare_inputs(k)
    p = k.p
    final = "l1f" in blocks
    out_t = p.dram("y_out", [T, D], F32, kind="ExternalOutput") if final \
        else p.dram("y_out_fm", [D, T], F32, kind="ExternalOutput")

    c = k.c
    c["X"] = p.sbuf("X", (128, NCH * T), F32)
    c["ones_bf"] = p.sbuf("ones_bf", (128, 1), BF16)
    c["ones_f1"] = p.sbuf("ones_f1", (1, 128), F32)
    c["rowscr"] = p.sbuf("rowscr", (1, 3 * 512), F32)
    c["psA"] = p.psum("psA", (128, 512), F32)
    c["psB"] = p.psum("psB", (128, 512), F32)
    c["psC"] = p.psum("psC", (128, 512), F32)
    c["psD"] = p.psum("psD", (128, 512), F32)
    c["psE"] = p.psum("psE", (128, 512), F32)
    c["psU"] = p.psum("psU", (128, 512), F32)
    c["mbc"] = p.sbuf("mbc", (128, 512), F32)
    c["rbc"] = p.sbuf("rbc", (128, 512), F32)
    c["eyef"] = p.sbuf("eyef_t", (128, 128), F32)
    k.ab = p.sbuf("arena_bf", (128, 45056), BF16)
    k.af = p.sbuf("arena_f", (128, 10256), F32)
    X = c["X"]

    p.memset(c["ones_bf"].full, 1.0)
    p.memset(c["ones_f1"].full, 1.0)
    p.dma("sync", c["eyef"].full, k.inp["eyef"].full, tag="ld_eye")
    for ch in range(NCH):
        p.dma("sync", X[:, ch * T:(ch + 1) * T],
              k.inp["x_fm"][ch * 128:(ch + 1) * 128, :], tag="ld_x")

    if "l0a" in blocks:
        emit_l0_attn(k)
    if "l0f" in blocks:
        emit_l0_ffn(k)
    if "l1a" in blocks:
        emit_l1_attn(k)
    if "l1f" in blocks:
        emit_l1_ffn(k, out_t)
    else:
        for ch in range(NCH):
            p.dma("sync", out_t[ch * 128:(ch + 1) * 128, :],
                  X[:, ch * T:(ch + 1) * T], tag="st_x")
    p.lower_and_emit()
    return nc



def emit_l0_attn(k):
    p, c = k.p, k.c
    X = c["X"]

    ln_g = p.sbuf("ln_g_t", (128, NCH), F32)
    ln_b = p.sbuf("ln_b_t", (128, NCH), F32)
    conv_b = p.sbuf("conv_b_t", (128, NCHI), F32)
    conv_w = p.sbuf("conv_w_t", (128, NCHI * 4), F32)
    b_dt = p.sbuf("b_dt_t", (128, NCHI), F32)
    d_vec = p.sbuf("d_vec_t", (128, NCHI), F32)
    a_log = p.sbuf("a_log_t", (1, 64), F32)
    for n, t in [("ln_g", ln_g), ("ln_b", ln_b), ("conv_b", conv_b),
                 ("conv_w", conv_w), ("b_dt", b_dt), ("d_vec", d_vec),
                 ("a_log", a_log)]:
        p.dma("sync", t.full, k.inp[n].full, tag="ld_" + n)
    k.reset_arenas()
    w_in_bf = p.dram("w_in_bf", [D, 4096], BF16)
    w_dt_bf = p.dram("w_dt_bf", [DI, DI], BF16)
    w_out_bf = p.dram("w_out_bf", [DI, D], BF16)
    p.dma("pool", w_in_bf, k.inp["w_in"], tag="pc_win")
    p.dma("pool", w_dt_bf, k.inp["w_dt"], tag="pc_wdt")
    p.dma("pool", w_out_bf, k.inp["w_out"], tag="pc_wout")
    w_x_bf = k.bcarve("w_x_bf", (128, NCHI * 128))
    p.dma("pool", w_x_bf.full, kmaj(k.inp["w_x"], 0, 128), tag="ld_wx")

    a_row = p.sbuf("a_row", (1, 64), F32)
    p.act(a_row.full, a_log.full, AF.Exp, tag="a_exp", fs=True)
    p.ts(a_row.full, a_row.full, -1.0, tag="a_neg", fs=True)

    win_g = [k.bcarve("win_g", (128, NCH * 512))]
    wdt_g = [k.bcarve("wdt_g", (128, NCHI * 256))]
    wout_g = [k.bcarve("wout_g", (128, NCHI * 128)),
              k.bcarve("wout_g2", (128, NCHI * 128))]
    c["xg_bf"] = k.bcarve("l0a_xg", (128, NCH * 512), at=win_g[0].f_off)
    c["sg_bf"] = k.bcarve("l0a_sg", (128, NCH * 512), at=wdt_g[0].f_off)

    xn_tt = k.bcarve("xn_tt", (128, NCH * TT))
    xp = k.bcarve("xp", (128, NCHI * (TT + 3)))
    res = k.bcarve("res_t", (128, NCHI * TT))
    xc = k.bcarve("xc", (128, NCHI * TT))
    ln1o = k.fcarve("ln1o", (128, TT))
    fscr = k.fcarve("fscr", (128, TT))
    r12bc = k.fcarve("r12bc", (128, TT))
    sp = fscr
    sp2 = ln1o
    dmacc = k.fcarve("dmacc", (1, TT))
    ad = k.fcarve("ad", (64, TT))
    bst = r12bc
    hsc = k.fcarve("hsc", (64, TT))
    hc_carry = k.fcarve("hc_carry", (64, 1))
    y_row = k.fcarve("y_row", (1, TT))
    c["lnS3"] = k.fcarve("lnS3", (1, 512))
    c["lnS4"] = k.fcarve("lnS4", (1, 512))
    spbf = k.bcarve("spbf", (128, 512))
    silu_r = c["rbc"]
    ybc = c["mbc"]
    p.memset(hc_carry.full, 0.0)

    for tt in range(NTT):
        t0 = tt * TT
        wb = tt % 2
        m_row, r1_row, r12_row = ln_group_stats(k, X, t0, True, f"ln1_{tt}")
        bc512(k, m_row, c["mbc"].full, "bm")
        bc512(k, r1_row, c["rbc"].full, "br")
        bc512(k, r12_row, r12bc.full, "b12")
        for ch in range(NCH):
            xsl = X[:, ch * T + t0: ch * T + t0 + TT]
            p.tt(ln1o.full, xsl, c["mbc"].full, ALU.subtract, tag="l1sub")
            p.tt(fscr.full, ln1o.full, r12bc[:, 0:512], ALU.mult, tag="l1r12")
            p.act(xn_tt[:, ch * TT:(ch + 1) * TT], fscr.full, AF.Identity,
                  bias=ln_b[:, ch:ch + 1], scale=ln_g[:, ch:ch + 1], tag="l1aff")
            p.tt(ln1o.full, ln1o.full, c["rbc"].full, ALU.mult, tag="l1r1")
            p.tt(xsl, xsl, ln1o.full, ALU.add, tag="l1add")
        for grp in range(8):
            n0 = grp * 512
            for hh in range(2):
                hof = hh * (NCH * 256)
                p.dma("sync", win_g[0][:, hof:hof + NCH * 256],
                      kmaj(w_in_bf, n0 + hh * 256, n0 + hh * 256 + 256),
                      tag="ld_win")
                for nl in range(2):
                    ncc = hh * 2 + nl
                    ps = (c["psA"], c["psB"], c["psC"], c["psD"])[ncc % 4]
                    for kc in range(NCH):
                        p.mm(ps[:, 0:512],
                             win_g[0][:, hof + kc * 256 + nl * 128:
                                     hof + kc * 256 + nl * 128 + 128],
                             xn_tt[:, kc * TT:(kc + 1) * TT],
                             start=(kc == 0), stop=(kc == NCH - 1), tag="mm_win")
                    ic = grp * 4 + ncc
                    if ic < NCHI:
                        p.copy(xp[:, ic * (TT + 3) + 3: (ic + 1) * (TT + 3)],
                               ps[:, 0:512], tag="ev_xp")
                    else:
                        p.copy(res[:, (ic - NCHI) * TT:(ic - NCHI + 1) * TT],
                               ps[:, 0:512], tag="ev_res")
        for ic in range(NCHI):
            b0 = ic * (TT + 3)
            if tt == 0:
                p.memset(xp[:, b0:b0 + 3], 0.0, tag="xp0", fs=True)
            p.act(sp.full, xp[:, b0:b0 + TT], AF.Copy,
                  scale=conv_w[:, ic * 4 + 0:ic * 4 + 1], tag="cv0")
            for j in (1, 2, 3):
                p.stt(sp.full, xp[:, b0 + j:b0 + j + TT],
                      conv_w[:, ic * 4 + j:ic * 4 + j + 1], sp.full,
                      ALU.mult, ALU.add, tag=f"cv{j}")
            p.act(xc[:, ic * TT:(ic + 1) * TT], sp.full, AF.Silu,
                  bias=conv_b[:, ic:ic + 1], tag="cv_silu")
        if tt < NTT - 1:
            for ic in range(NCHI):
                b0 = ic * (TT + 3)
                p.copy(xp[:, b0:b0 + 3], xp[:, b0 + TT:b0 + TT + 3], tag="xp_c", fs=True)
        p.memset(dmacc.full, 0.0, tag="dm0", fs=True)
        for grp in range(8):
            for ncc in range(2):
                hof = ncc * (NCHI * 128)
                p.dma("sync", wdt_g[0][:, hof:hof + NCHI * 128],
                      kmaj(w_dt_bf, grp * 256 + ncc * 128,
                           grp * 256 + (ncc + 1) * 128), tag="ld_wdt")
                ic = grp * 2 + ncc
                ps = (c["psA"], c["psB"], c["psC"], c["psD"])[ic % 4]
                for kc in range(NCHI):
                    p.mm(ps[:, 0:512],
                         wdt_g[0][:, hof + kc * 128:hof + (kc + 1) * 128],
                         xc[:, kc * TT:(kc + 1) * TT],
                         start=(kc == 0), stop=(kc == NCHI - 1), tag="mm_wdt")
                p.act(sp.full, ps[:, 0:512], AF.Abs, bias=b_dt[:, ic:ic + 1],
                      tag="spA")
                p.act(sp.full, sp.full, AF.Exp, scale=-1.0, tag="spE")
                p.act(sp.full, sp.full, AF.Ln, bias=1.0, tag="spL")
                p.act(sp2.full, ps[:, 0:512], AF.Relu, bias=b_dt[:, ic:ic + 1],
                      tag="spR")
                p.tt(sp2.full, sp2.full, sp.full, ALU.add, tag="spS")
                p.copy(spbf.full, sp2.full, tag="spB")
                p.mm(c["psU"][0:1, 0:512], c["ones_bf"][:, 0:1],
                     spbf.full, start=True, stop=True, tag="mm_dm")
                p.tt(dmacc.full, dmacc.full, c["psU"][0:1, 0:512], ALU.add,
                     tag="dm_a", fs=True)
        p.ts(dmacc.full, dmacc.full, 1.0 / DI, tag="dm_s", fs=True)
        p.mm(c["psU"][0:64, 0:512], a_row.full, dmacc.full, start=True, stop=True,
             tag="mm_ad")
        p.act(ad.full, c["psU"][0:64, 0:512], AF.Exp, tag="ad_e")
        bstB = bst[0:64, 0:512]
        bstC = c["rbc"][0:64, 0:512]
        for kc in range(NCHI):
            p.mm(c["psA"][0:64, 0:512], w_x_bf[:, kc * 128:kc * 128 + 64],
                 xc[:, kc * TT:(kc + 1) * TT],
                 start=(kc == 0), stop=(kc == NCHI - 1), tag="mm_wxB")
        p.copy(bstB, c["psA"][0:64, 0:512], tag="ev_bcB")
        for kc in range(NCHI):
            p.mm(c["psB"][0:64, 0:512], w_x_bf[:, kc * 128 + 64:(kc + 1) * 128],
                 xc[:, kc * TT:(kc + 1) * TT],
                 start=(kc == 0), stop=(kc == NCHI - 1), tag="mm_wxC")
        p.copy(bstC, c["psB"][0:64, 0:512], tag="ev_bcC")
        p.scan(hsc.full, ad.full, bstB, hc_carry[:, 0:1], ALU.mult,
               ALU.add, tag="scan_h")
        p.copy(hc_carry.full, hsc[:, TT - 1:TT], tag="h_cr", fs=True)
        p.tt(hsc.full, hsc.full, bstC, ALU.mult, tag="h_c")
        p.copy(spbf[0:64, 0:512], hsc.full, tag="hc_b")
        p.mm(c["psU"][0:1, 0:512], c["ones_bf"][0:64, 0:1],
             spbf[0:64, 0:512], start=True, stop=True, tag="mm_yr")
        p.copy(y_row.full, c["psU"][0:1, 0:512], tag="ev_yr", fs=True)
        p.mm(c["psE"][:, 0:512], c["ones_f1"].full, y_row.full, start=True,
             stop=True, tag="bc_y")
        p.copy(ybc.full, c["psE"][:, 0:512], tag="bc_yc")
        for ic in range(NCHI):
            p.act(silu_r.full, res[:, ic * TT:(ic + 1) * TT], AF.Silu, tag="y_si")
            p.stt(sp.full, xc[:, ic * TT:(ic + 1) * TT], d_vec[:, ic:ic + 1],
                  ybc.full, ALU.mult, ALU.add, tag="y_dx")
            p.tt(xc[:, ic * TT:(ic + 1) * TT], sp.full, silu_r.full, ALU.mult,
                 tag="y_m")
        for oc in range(NCH):
            wg = wout_g[oc % 2]
            p.dma("sync", wg.full,
                  kmaj(w_out_bf, oc * 128, (oc + 1) * 128), tag="ld_wo")
            ps = c["psB"] if oc % 2 == 0 else c["psC"]
            for kc in range(NCHI):
                p.mm(ps[:, 0:512], wg[:, kc * 128:(kc + 1) * 128],
                     xc[:, kc * TT:(kc + 1) * TT],
                     start=(kc == 0), stop=(kc == NCHI - 1), tag="mm_wo")
            xsl = X[:, oc * T + t0: oc * T + t0 + TT]
            p.tt(xsl, xsl, ps[:, 0:512], ALU.add, tag="res_add")


def emit_l0_ffn(k):
    p, c = k.p, k.c
    X = c["X"]
    NI = 8
    k.reset_arenas()
    Ch = k.bcarve("Ch", (128, NI * 512))
    Sb = k.bcarve("Sb", (128, NI * 1024))
    wrb = k.bcarve("wrb", (128, NI * 1024))
    wib = k.bcarve("wib", (128, NI * 1024))
    P1T = k.bcarve("P1T", (128, NI * 1024))
    P2TN = k.bcarve("P2TN", (128, NI * 1024))
    xn2 = k.bcarve("xn2", (128, NCH * 512), at=wib.f_off)
    c["xg_bf"] = k.bcarve("l0f_xg", (128, NCH * 512), at=P1T.f_off)
    c["sg_bf"] = k.bcarve("l0f_sg", (128, NCH * 512), at=P2TN.f_off)
    fb = p.sbuf("fb_t", (128, NCH), F32)
    f2scr = k.fcarve("f2scr", (128, 512))

    def ldC(half):
        return p.dma("pool", Ch.full,
                     k.inp["tC"].rv(lambda h=half: k.inp["tC"].h.ap()
                                    [:, h * 512:(h + 1) * 512]
                                    .rearrange("(kc p) n -> p kc n", p=128)),
                     tag="ld_C")

    p.dma("pool", Sb.full, kmaj(k.inp["tS"], 0, 1024), tag="ld_S")
    p.dma("pool", wrb.full, kmaj(k.inp["wfr"], 0, 1024), tag="ld_wr")
    p.dma("pool", wib.full, kmaj(k.inp["wfi"], 0, 1024), tag="ld_wi")
    p.dma("sync", fb.full, k.inp["fbias"].full, tag="ld_fb")

    def st1(dst, lA, rA, lB, rB, scale, tag, rA_half=False, rB_half=False):
        for ih in range(2):
            if rA_half or rB_half:
                ldC(ih)
            for oc in range(NI):
                ps = c["psA"] if oc % 2 == 0 else c["psB"]
                for d in range(NI):
                    rr = rA[:, d * 512:(d + 1) * 512] if rA_half else                         rA[:, d * 1024 + ih * 512: d * 1024 + ih * 512 + 512]
                    p.mm(ps[:, 0:512],
                         lA[:, d * 1024 + oc * 128: d * 1024 + oc * 128 + 128],
                         rr, start=(d == 0), stop=False, tag=tag + "a")
                for d in range(NI):
                    rr = rB[:, d * 512:(d + 1) * 512] if rB_half else                         rB[:, d * 1024 + ih * 512: d * 1024 + ih * 512 + 512]
                    p.mm(ps[:, 0:512],
                         lB[:, d * 1024 + oc * 128: d * 1024 + oc * 128 + 128],
                         rr, start=False, stop=(d == NI - 1), tag=tag + "b")
                p.act(dst[:, oc * 1024 + ih * 512: oc * 1024 + ih * 512 + 512],
                      ps[:, 0:512], AF.Copy, scale=scale, tag=tag + "ev")

    st1(P1T, wrb, Ch, wib, Sb, 1.0, "p1", rA_half=True)
    for d in range(NI):
        p.act(wib[:, d * 1024:(d + 1) * 1024], wib[:, d * 1024:(d + 1) * 1024],
              AF.Copy, scale=-1.0, tag="wineg")
    st1(P2TN, wrb, Sb, wib, Ch, 1.0, "p2", rB_half=True)
    Mt = wrb
    st1(Mt, P1T, Ch, P2TN, Sb, 1.0 / D, "m2", rA_half=True)

    for g in range(4):
        t0 = g * 512
        m_row, r1_row, _ = ln_group_stats(k, X, t0, False, f"ln2_{g}")
        bc512(k, m_row, c["mbc"].full, "f2m")
        bc512(k, r1_row, c["rbc"].full, "f2r")
        for ch in range(NCH):
            xsl = X[:, ch * T + t0: ch * T + t0 + 512]
            p.tt(f2scr.full, xsl, c["mbc"].full, ALU.subtract, tag="f2sub")
            p.tt(xn2[:, ch * 512:(ch + 1) * 512], f2scr.full, c["rbc"].full,
                 ALU.mult, tag="f2mul")
        for jc in range(NCH):
            ps = c["psC"] if jc % 2 == 0 else c["psD"]
            for icn in range(NI):
                p.mm(ps[:, 0:512],
                     Mt[:, icn * 1024 + jc * 128: icn * 1024 + jc * 128 + 128],
                     xn2[:, icn * 512:(icn + 1) * 512],
                     start=(icn == 0), stop=(icn == NI - 1), tag="mm_y2")
            xsl = X[:, jc * T + t0: jc * T + t0 + 512]
            p.stt(xsl, ps[:, 0:512], fb[:, jc:jc + 1], xsl, ALU.add, ALU.add,
                  tag="f2res")


def emit_l1_attn(k):
    p, c = k.p, k.c
    X = c["X"]
    HL = 1024
    prm = {}
    for n in ("tdec", "tfirst", "tmk", "tmv", "tmr"):
        prm[n] = p.sbuf(n + "_t", (128, NCH), F32)
        p.dma("sync", prm[n].full, k.inp[n].full, tag="ld_" + n)
    e1 = p.sbuf("e1_t", (128, NCH), F32)
    wn = p.sbuf("wn_t", (128, NCH), F32)
    ew = p.sbuf("ew_t", (128, NCH), F32)
    eu = p.sbuf("eu_t", (128, NCH), F32)
    p.act(e1.full, prm["tdec"].full, AF.Exp, tag="wk_e1", fs=True)
    p.ts(wn.full, e1.full, -1.0, tag="wk_wn", fs=True)
    p.act(ew.full, e1.full, AF.Exp, scale=-1.0, tag="wk_ew", fs=True)
    p.act(eu.full, prm["tfirst"].full, AF.Exp, tag="wk_eu", fs=True)

    k.reset_arenas()
    c["xg_bf"] = k.bcarve("l1a_xg", (128, NCH * 512))
    c["sg_bf"] = k.bcarve("l1a_sg", (128, NCH * 512))
    xn3h = k.bcarve("xn3h", (128, NCH * (HL + 1)))
    ry_bf = xn3h
    xk_bf = k.bcarve("xk_bf", (128, NCH * HL))
    xv_bf = k.bcarve("xv_bf", (128, NCH * HL))
    xr_bf = k.bcarve("xr_bf", (128, NCH * HL))
    xcar = k.bcarve("xcar", (128, NCH))
    wbuf = [k.bcarve(f"l1w{i}", (128, NCH * 128)) for i in range(2)]
    kscan = k.fcarve("kscan", (128, HL + 1))
    ket = k.fcarve("ket", (128, HL + 1))
    kvet = k.fcarve("kvet", (128, HL + 1))
    ewt = k.fcarve("ewt", (128, HL + 1))
    wlt = k.fcarve("wlt", (128, HL + 1))
    asc = k.fcarve("asc", (128, HL + 1))
    bsc = k.fcarve("bsc", (128, HL + 1))
    psc = k.fcarve("psc", (128, HL + 1))
    onesl = k.fcarve("onesl", (128, HL + 1))
    mscr = k.fcarve("mscr", (128, HL))
    rsig = mscr
    acar = p.sbuf("acar", (128, NCH), F32)
    bcar = p.sbuf("bcar", (128, NCH), F32)
    pcar = p.sbuf("pcar", (128, NCH), F32)
    p.memset(onesl.full, 1.0)
    p.memset(acar.full, 0.0)
    p.memset(bcar.full, 0.0)
    p.memset(pcar.full, -1e30)

    nwl = [0]

    def wload(name, cc):
        b = wbuf[nwl[0] % 2]
        nwl[0] += 1
        p.dma("pool", b.full, kmaj(k.inp[name], cc * 128, (cc + 1) * 128),
              tag="ld_" + name)
        return b

    for h in range(2):
        t0 = h * HL
        for ch in range(NCH):
            b0 = ch * (HL + 1)
            if h == 0:
                p.memset(xn3h[:, b0:b0 + 1], 0.0, tag="x3z", fs=True)
            else:
                p.copy(xn3h[:, b0:b0 + 1], xcar[:, ch:ch + 1], tag="x3c", fs=True)
        for g in range(2):
            m_row, r1_row, _ = ln_group_stats(k, X, t0 + g * 512, False,
                                              f"ln3_{h}{g}")
            bc512(k, m_row, c["mbc"].full, "l3m")
            bc512(k, r1_row, c["rbc"].full, "l3r")
            for ch in range(NCH):
                xsl = X[:, ch * T + t0 + g * 512: ch * T + t0 + g * 512 + 512]
                p.tt(mscr[:, 0:512], xsl, c["mbc"].full, ALU.subtract, tag="l3sub")
                p.tt(xn3h[:, ch * (HL + 1) + 1 + g * 512:
                          ch * (HL + 1) + 1 + g * 512 + 512],
                     mscr[:, 0:512], c["rbc"].full, ALU.mult, tag="l3mul")
        for ch in range(NCH):
            p.copy(xcar[:, ch:ch + 1], xn3h[:, ch * (HL + 1) + HL:
                                            ch * (HL + 1) + HL + 1], tag="x3s", fs=True)
        for ch in range(NCH):
            b0 = ch * (HL + 1)
            xv_ = xn3h[:, b0 + 1:b0 + 1 + HL]
            xx_ = xn3h[:, b0:b0 + HL]
            p.tt(mscr.full, xv_, xx_, ALU.subtract, eng="pool", tag="mixd")
            for nm, dst in (("tmk", xk_bf), ("tmv", xv_bf), ("tmr", xr_bf)):
                p.stt(dst[:, ch * HL:(ch + 1) * HL], mscr.full,
                      prm[nm][:, ch:ch + 1], xx_, ALU.mult, ALU.add,
                      tag="mix" + nm)
        for cc in range(NCH):
            ecol = lambda t: t[:, cc:cc + 1]
            wk = wload("wk_t", cc)
            for g in range(2):
                ps = c["psA"] if g == 0 else c["psB"]
                for kc in range(NCH):
                    p.mm(ps[:, 0:512], wk[:, kc * 128:(kc + 1) * 128],
                         xk_bf[:, kc * HL + g * 512: kc * HL + g * 512 + 512],
                         start=(kc == 0), stop=(kc == NCH - 1), tag="mm_k")
                p.copy(kscan[:, 1 + g * 512:1 + g * 512 + 512], ps[:, 0:512],
                       tag="ev_k")
            wv = wload("wv_t", cc)
            p.memset(ket[:, 0:1], 0.0, tag="ke0")
            p.act(ket[:, 1:HL + 1], kscan[:, 1:HL + 1], AF.Exp, tag="ke_e")
            p.memset(kvet[:, 0:1], 0.0, tag="kve0")
            for g in range(2):
                ps = c["psC"] if g == 0 else c["psD"]
                for kc in range(NCH):
                    p.mm(ps[:, 0:512], wv[:, kc * 128:(kc + 1) * 128],
                         xv_bf[:, kc * HL + g * 512: kc * HL + g * 512 + 512],
                         start=(kc == 0), stop=(kc == NCH - 1), tag="mm_v")
                p.tt(kvet[:, 1 + g * 512:1 + g * 512 + 512],
                     ket[:, 1 + g * 512:1 + g * 512 + 512], ps[:, 0:512],
                     ALU.mult, tag="kve_m")
            wr = wload("wr_t", cc)
            for g in range(2):
                ps = c["psE"] if g == 0 else c["psU"]
                for kc in range(NCH):
                    p.mm(ps[:, 0:512], wr[:, kc * 128:(kc + 1) * 128],
                         xr_bf[:, kc * HL + g * 512: kc * HL + g * 512 + 512],
                         start=(kc == 0), stop=(kc == NCH - 1), tag="mm_r")
                p.act(rsig[:, g * 512:g * 512 + 512], ps[:, 0:512], AF.Sigmoid,
                      tag="r_sig")
            p.memset(ewt[:, 0:1], 1.0, tag="ew1")
            p.act(ewt[:, 1:HL + 1], onesl[:, 1:HL + 1], AF.Copy, scale=ecol(ew),
                  tag="ew_b")
            p.memset(wlt[:, 0:1], 0.0, tag="wl0")
            p.act(wlt[:, 1:HL + 1], onesl[:, 1:HL + 1], AF.Copy, scale=ecol(wn),
                  tag="wl_b")
            p.memset(kscan[:, 0:1], -3e38, tag="ks0")
            p.scan(asc.full, ewt.full, kvet.full, ecol(acar), ALU.mult, ALU.add,
                   tag="sc_a")
            p.scan(bsc.full, ewt.full, ket.full, ecol(bcar), ALU.mult, ALU.add,
                   tag="sc_b")
            p.scan(psc.full, wlt.full, kscan.full, ecol(pcar), ALU.add, ALU.max,
                   tag="sc_p")
            p.copy(acar[:, cc:cc + 1], asc[:, HL:HL + 1], tag="cr_a", fs=True)
            p.copy(bcar[:, cc:cc + 1], bsc[:, HL:HL + 1], tag="cr_b", fs=True)
            p.copy(pcar[:, cc:cc + 1], psc[:, HL:HL + 1], tag="cr_p", fs=True)
            aev = asc[:, 0:HL]
            bev = bsc[:, 0:HL]
            pev = psc[:, 0:HL]
            p.stt(aev, aev, ecol(eu), kvet[:, 1:HL + 1], ALU.mult, ALU.add,
                  tag="num")
            p.stt(bev, bev, ecol(eu), ket[:, 1:HL + 1], ALU.mult, ALU.add,
                  tag="den")
            p.stt(pev, pev, ecol(prm["tfirst"]), kscan[:, 1:HL + 1], ALU.add,
                  ALU.max, tag="pt")
            p.act(pev, pev, AF.Exp, tag="ep")
            p.stt(pev, pev, 1e-6, bev, ALU.mult, ALU.add, tag="den2")
            p.act(pev, pev, AF.Ln, tag="rcpL")
            p.act(pev, pev, AF.Exp, scale=-1.0, tag="rcpE")
            p.tt(aev, aev, pev, ALU.mult, tag="y")
            p.tt(ry_bf[:, cc * HL:(cc + 1) * HL], aev, rsig.full, ALU.mult,
                 tag="ry")
        for oc in range(NCH):
            wo = wload("wo_t", oc)
            for g in range(2):
                ps = c["psC"] if g == 0 else c["psD"]
                for kc in range(NCH):
                    p.mm(ps[:, 0:512], wo[:, kc * 128:(kc + 1) * 128],
                         ry_bf[:, kc * HL + g * 512: kc * HL + g * 512 + 512],
                         start=(kc == 0), stop=(kc == NCH - 1), tag="mm_wo1")
                xsl = X[:, oc * T + t0 + g * 512: oc * T + t0 + g * 512 + 512]
                p.tt(xsl, xsl, ps[:, 0:512], ALU.add, tag="res1_add")


def emit_l1_ffn(k, out_t):
    p, c = k.p, k.c
    X = c["X"]
    NI32 = 32
    k.reset_arenas()

    rw_t = k.fcarve("rw_t", (128, NCH * NEXP))
    for kc in range(NCH):
        p.dma("sync", rw_t[:, kc * NEXP:(kc + 1) * NEXP],
              k.inp["rw"][kc * 128:(kc + 1) * 128, :], tag="ld_rw")
    rb_sb = p.sbuf("rb_sb", (1, 128), F32)
    p.dma("sync", rb_sb.full, k.inp["rb16"].full, tag="ld_rb")
    pri = p.sbuf("pri_t", (128, 128), F32)
    p.dma("sync", pri.full, k.inp["pri16"].full, tag="ld_pri")
    tokid = p.sbuf("tokid_t", (128, 16), F32)
    p.dma("sync", tokid.full, k.inp["tokid"].full, tag="ld_tok")
    for n in ("b1c", "b2c", "gec"):
        k.c[n] = p.sbuf(n + "_t", (128, k.inp[n].shape[1]), F32)
        p.dma("sync", k.c[n].full, k.inp[n].full, tag="ld_" + n)
    gec_s = k.c["gec"]
    p.ts(gec_s.full, gec_s.full, 0.1, tag="gec_s", fs=True)
    eyeb = p.sbuf("eyeb", (128, 128), BF16)
    p.copy(eyeb.full, c["eyef"].full, tag="eyeb")

    c["xg_bf"] = k.bcarve("l1f_xg", (128, NCH * 512))
    c["sg_bf"] = k.bcarve("l1f_sg", (128, NCH * 512))
    xn4_bf = k.bcarve("xn4_bf", (128, NCH * 512))
    x4scr = k.fcarve("x4scr", (128, 512))
    LGs = k.fcarve("LGs", (128, 128))
    p.memset(LGs.full, 0.0, tag="lg0")
    xn4_tm = p.dram("xn4_tm", [T, D], BF16)
    y_moe = p.dram("y_moe", [T, D], BF16)
    slot_src = p.dram("slot_src", [NEXP * CAPE, 1], F32)
    pstb = p.psum("pstb", (128, 128), BF16)

    x4t_sb = k.bcarve("x4t_sb", (128, D))
    for g in range(4):
        t0 = g * 512
        m_row, r1_row, _ = ln_group_stats(k, X, t0, False, f"ln4_{g}")
        bc512(k, m_row, c["mbc"].full, "l4m")
        bc512(k, r1_row, c["rbc"].full, "l4r")
        for ch in range(NCH):
            xsl = X[:, ch * T + t0: ch * T + t0 + 512]
            p.tt(x4scr.full, xsl, c["mbc"].full, ALU.subtract, tag="l4sub")
            p.tt(x4scr.full, x4scr.full, c["rbc"].full, ALU.mult, tag="l4mul")
            p.copy(xn4_bf[:, ch * 512:(ch + 1) * 512], x4scr.full, tag="l4bf")
            for tsl in range(4):
                ts = g * 4 + tsl
                p.mm(c["psU"][:, 0:NEXP],
                     x4scr[:, tsl * 128:(tsl + 1) * 128],
                     rw_t[:, ch * NEXP:(ch + 1) * NEXP],
                     start=True, stop=True, tag="mm_rt")
                p.tt(LGs[:, ts * NEXP:(ts + 1) * NEXP],
                     LGs[:, ts * NEXP:(ts + 1) * NEXP], c["psU"][:, 0:NEXP],
                     ALU.add, tag="lg_a", fs=True)
        for tbl in range(4):
            tb = g * 4 + tbl
            for ch in range(NCH):
                p.transpose(pstb.full,
                            xn4_bf[:, ch * 512 + tbl * 128:
                                   ch * 512 + (tbl + 1) * 128],
                            eyeb.full, tag="tr_x4")
                p.copy(x4t_sb[:, ch * 128:(ch + 1) * 128], pstb.full, tag="ev_x4")
            p.dma("sync", xn4_tm[tb * 128:(tb + 1) * 128, :], x4t_sb.full,
                  tag="st_x4")
    p.mm(c["psU"][:, 0:128], c["ones_f1"].full, rb_sb.full, start=True, stop=True,
         tag="mm_rb")
    p.tt(LGs.full, LGs.full, c["psU"][:, 0:128], ALU.add, tag="lg_b", fs=True)

    mx = k.fcarve("mx_t", (128, 16))
    eq = k.fcarve("eq_t", (128, 8))
    mxb8 = k.fcarve("mxb8_t", (128, 8))
    ones17 = k.fcarve("ones17", (128, 17))
    p.memset(ones17.full, 1.0, tag="o17", fs=True)
    idxf = k.fcarve("idxf_t", (128, 16))
    for g in range(16):
        lgg = LGs[:, g * NEXP:(g + 1) * NEXP]
        p.red(mx[:, g:g + 1], lgg, ALU.max, tag="am_r", fs=True)
        p.ts(mxb8.full, ones17[:, 0:8], mx[:, g:g + 1], None, ALU.mult,
             tag="am_b", fs=True)
        p.tt(eq.full, lgg, mxb8.full, ALU.subtract, tag="am_d", fs=True)
        p.ts(eq.full, eq.full, 0.0, None, ALU.is_equal, tag="am_e", fs=True)
        p.tt(eq.full, eq.full, pri[:, 0:NEXP], ALU.mult, tag="am_p", fs=True)
        p.red(idxf[:, g:g + 1], eq.full, ALU.max, tag="am_r2", fs=True)
    p.ts(idxf.full, idxf.full, -1.0, 8.0, ALU.mult, ALU.add, tag="am_i", fs=True)

    me = k.fcarve("me_t", (128, 17))
    rsc = k.fcarve("rsc_t", (128, 17))
    tote = k.fcarve("tote", (128, NEXP))
    pofs = k.fcarve("pofs", (128, NEXP))
    dacc = k.fcarve("dacc", (128, 16))
    dscr = k.fcarve("dscr", (128, 16))
    ofl = k.fcarve("ofl", (128, 16))
    rank_e = []
    p.memset(dacc.full, 0.0, tag="d0", fs=True)
    for e_ in range(NEXP):
        p.ts(me[:, 1:17], idxf.full, float(e_), None, ALU.is_equal, tag="me", fs=True)
        p.memset(me[:, 0:1], 0.0, tag="me0", fs=True)
        p.scan(rsc.full, ones17.full, me.full, 0.0, ALU.mult, ALU.add, tag="rk")
        p.copy(tote[:, e_:e_ + 1], rsc[:, 16:17], tag="tot", fs=True)
        re = k.fcarve(f"rke{e_}", (128, 16))
        rank_e.append(re)
        p.copy(re.full, rsc[:, 0:16], tag="rkc", fs=True)
    p.transpose(c["psU"][0:NEXP, 0:128], tote.full, c["eyef"].full, tag="tr_tot")
    totT = k.fcarve("totT", (NEXP, 129))
    p.memset(totT[:, 0:1], 0.0, tag="tT0", fs=True)
    p.copy(totT[0:NEXP, 1:129], c["psU"][0:NEXP, 0:128], tag="tTc", fs=True)
    onesP = k.fcarve("onesP", (NEXP, 129))
    p.memset(onesP.full, 1.0, tag="oP")
    cofT = k.fcarve("cofT", (NEXP, 129))
    p.scan(cofT.full, onesP.full, totT.full, 0.0, ALU.mult, ALU.add, tag="sc_t")
    p.transpose(c["psU"][0:128, 0:NEXP], cofT[0:NEXP, 0:128],
                c["eyef"][0:NEXP, 0:NEXP], tag="tr_cof")
    p.copy(pofs.full, c["psU"][0:128, 0:NEXP], tag="pofs_c", fs=True)
    for e_ in range(NEXP):
        p.ts(dscr.full, rank_e[e_].full, pofs[:, e_:e_ + 1],
             float(CAPE * e_), ALU.add, ALU.add, tag="dst1", fs=True)
        p.ts(ofl.full, dscr.full, float(1 - CAPE * (e_ + 1)), None, ALU.add,
             tag="dst2a", fs=True)
        p.relu(ofl.full, ofl.full, tag="dst2b", fs=True)
        p.stt(dscr.full, ofl.full, 1e6, dscr.full, ALU.mult, ALU.add, tag="dst3", fs=True)
        p.ts(me[:, 1:17], idxf.full, float(e_), None, ALU.is_equal, tag="dst4", fs=True)
        p.tt(dscr.full, dscr.full, me[:, 1:17], ALU.mult, tag="dst5", fs=True)
        p.tt(dacc.full, dacc.full, dscr.full, ALU.add, tag="dst6", fs=True)
    di32 = p.sbuf("di32", (128, 16), I32)
    p.copy(di32.full, dacc.full, tag="dcast", fs=True)

    islot = k.fcarve("islot", (128, 32))
    p.memset(islot.full, 1e9, tag="sl9")
    p.dma("pool", slot_src.rv(lambda: slot_src.h.ap()
                              .rearrange("(a b) o -> a (b o)", a=128)),
          islot.full, tag="sl_init")
    for j in range(16):
        p.iscatter(slot_src, tokid[:, j:j + 1], di32[:, j:j + 1],
                   NEXP * CAPE - 1, tag="sl_sc")
    slotf = k.fcarve("slotf", (128, 32))
    for cc in range(32):
        p.dma("pool", slotf[:, cc:cc + 1],
              slot_src[cc * 128:(cc + 1) * 128, :], tag="sl_rd")
    si32 = p.sbuf("si32", (128, 32), I32)
    p.copy(si32.full, slotf.full, tag="scast", fs=True)

    ymo_sb = k.bcarve("ymo_sb", (128, D))
    p.memset(ymo_sb.full, 0.0, tag="ym0")
    for tb in range(16):
        p.dma("sync", y_moe[tb * 128:(tb + 1) * 128, :], ymo_sb.full,
              tag="ym_init")

    xg_tm = k.bcarve("xg_tm", (128, 4 * D), at=0)
    xsel = k.bcarve("xsel", (128, NCH * CAPE), at=4096)
    h_bf = k.bcarve("h_bf", (128, NI32 * CAPE))
    w1g = k.bcarve("w1g", (128, NCH * 512))
    w2g = k.bcarve("w2g", (128, NI32 * 128))
    ytmp = k.fcarve("ytmp", (128, CAPE))
    yg_tm = k.bcarve("yg_tm", (128, 4 * D))
    for e_ in range(NEXP):
        for b in range(4):
            p.igather(xg_tm[:, b * D:(b + 1) * D], xn4_tm,
                      si32[:, e_ * 4 + b:e_ * 4 + b + 1], bound=T - 1, tag="gx")
            for ch in range(NCH):
                p.transpose(pstb.full,
                            xg_tm[:, b * D + ch * 128: b * D + (ch + 1) * 128],
                            eyeb.full, tag="tr_xs")
                p.act(xsel[:, ch * CAPE + b * 128: ch * CAPE + (b + 1) * 128],
                      pstb.full, AF.Identity,
                      bias=gec_s[:, e_ * NCH + ch:e_ * NCH + ch + 1], tag="ev_xs")
        for ig in range(8):
            for hh in range(2):
                hof = hh * (NCH * 256)
                p.dma("pool", w1g[:, hof:hof + NCH * 256],
                      kmajr(k.inp["moe_w1"], e_ * D, (e_ + 1) * D,
                            ig * 512 + hh * 256, ig * 512 + (hh + 1) * 256),
                      tag="ld_w1")
                for nl in range(2):
                    ic4 = hh * 2 + nl
                    ic = ig * 4 + ic4
                    ps = (c["psA"], c["psB"], c["psE"], c["psD"])[ic4 % 4]
                    for kc in range(NCH):
                        p.mm(ps[:, 0:512],
                             w1g[:, hof + kc * 256 + nl * 128:
                                 hof + kc * 256 + nl * 128 + 128],
                             xsel[:, kc * CAPE:(kc + 1) * CAPE],
                             start=(kc == 0), stop=(kc == NCH - 1), tag="mm_h")
                    p.act(h_bf[:, ic * CAPE:(ic + 1) * CAPE], ps[:, 0:512],
                          AF.Gelu,
                          bias=k.c["b1c"][:, e_ * 32 + ic:e_ * 32 + ic + 1],
                          tag="ev_h")
        for oc in range(NCH):
            ps = c["psC"] if oc % 2 == 0 else c["psD"]
            for hh in range(2):
                hof = hh * (16 * 128)
                p.dma("pool", w2g[:, hof:hof + 16 * 128],
                      kmajr(k.inp["moe_w2"], e_ * 4096 + hh * 2048,
                            e_ * 4096 + (hh + 1) * 2048,
                            oc * 128, (oc + 1) * 128), tag="ld_w2")
                for kl in range(16):
                    kc = hh * 16 + kl
                    p.mm(ps[:, 0:512], w2g[:, hof + kl * 128:hof + (kl + 1) * 128],
                         h_bf[:, kc * CAPE:(kc + 1) * CAPE],
                         start=(kc == 0), stop=(kc == NI32 - 1), tag="mm_y")
            p.act(ytmp.full, ps[:, 0:512], AF.Identity,
                  bias=k.c["b2c"][:, e_ * NCH + oc:e_ * NCH + oc + 1], tag="ev_y")
            for b in range(4):
                p.transpose(c["psU"][0:128, 0:128],
                            ytmp[:, b * 128:(b + 1) * 128], c["eyef"].full,
                            tag="tr_y")
                p.copy(yg_tm[:, b * D + oc * 128: b * D + (oc + 1) * 128],
                       c["psU"][0:128, 0:128], tag="ev_yt")
        for b in range(4):
            p.iscatter(y_moe, yg_tm[:, b * D:(b + 1) * D],
                       si32[:, e_ * 4 + b:e_ * 4 + b + 1], T - 1, tag="sc_y")

    if DEBUG_MOE:
        for nm, t_ in [("dbg_lgs", LGs), ("dbg_idx", idxf), ("dbg_dacc", dacc),
                       ("dbg_slotf", slotf), ("dbg_tote", tote),
                       ("dbg_pofs", pofs), ("dbg_mx", mx)]:
            o = p.dram(nm, list(t_.shape), F32, kind="ExternalOutput")
            p.dma("sync", o.full, t_.full, tag="dbg")

    out_sb = k.fcarve("out_sb", (128, D))
    for tb in range(16):
        p.dma("pool", ymo_sb.full, y_moe[tb * 128:(tb + 1) * 128, :], tag="ld_ym")
        for ch in range(NCH):
            p.transpose(c["psU"][0:128, 0:128],
                        X[:, ch * T + tb * 128: ch * T + (tb + 1) * 128],
                        c["eyef"].full, tag="tr_o")
            p.tt(out_sb[:, ch * 128:(ch + 1) * 128],
                 ymo_sb[:, ch * 128:(ch + 1) * 128], c["psU"][0:128, 0:128],
                 ALU.add, tag="o_add")
        p.dma("sync", out_t[tb * 128:(tb + 1) * 128, :], out_sb.full, tag="st_o")



BLOCK_SETS = [("l0a", "l0f", "l1a", "l1f")]


def kernel(**inputs):
    in_maps = _prep_inputs(inputs)
    x = None
    for bs in BLOCK_SETS:
        nc = build_program(bs)
        if x is not None:
            for ci in range(NCORES):
                in_maps[ci]["x_fm"] = np.ascontiguousarray(x[ci])
        res = run_bass_kernel_spmd(nc, in_maps, core_ids=list(range(NCORES)))
        if "l1f" in bs:
            return np.stack([np.asarray(r["y_out"]) for r in res.results], 0)
        x = np.stack([np.asarray(r["y_out_fm"]) for r in res.results], 0)
    return np.stack([xi.T for xi in x], 0)
